# version 18
# baseline (speedup 1.0000x reference)
import numpy as np
from contextlib import ExitStack

import concourse.bass as bass
import concourse.mybir as mybir
from concourse.bass_utils import run_bass_kernel_spmd

F32 = mybir.dt.float32
BF16 = mybir.dt.bfloat16
AF = mybir.ActivationFunctionType
ALU = mybir.AluOpType
AX = mybir.AxisListType

D = 2048
DC = 16
NH = 16
DH = 128
SEQ = 2048
TC = 16
LC = 8
B = 4
NCORE = 8
NE = 8
NEU = 10
HC = 8
MC = 96
SCALE = float(1.0 / np.sqrt(128.0))

ENGS = ("tensor", "vector", "scalar", "sync", "gpsimd")
PHASES = 3
SUB1 = 2


class Em:

    def __init__(self, nc, esem):
        self.nc = nc
        self.esem = esem
        self.cnt = {e: 0 for e in ENGS}
        self.ops = {e: [] for e in ENGS}

    def op(self, eng, fn, waits=(), inc=False, dma_inc=None):
        if inc:
            self.cnt[eng] += 1
        sem_h = self.esem[eng]
        wl = tuple(w for w in waits if w is not None)

        def emit(E, fn=fn, wl=wl, inc=inc, sem_h=sem_h, dma_inc=dma_inc):
            for (s, v) in wl:
                E.wait_ge(s, v)
            ins = fn(E)
            if inc:
                ins.then_inc(sem_h, 1)
            elif dma_inc is not None:
                ins.then_inc(dma_inc[0], dma_inc[1])

        self.ops[eng].append(emit)
        return (sem_h, self.cnt[eng])

    def drain(self, eng):
        self.ops[eng].append(lambda E: E.drain())

    def mark(self, eng):
        return (self.esem[eng], self.cnt[eng])

    def run_block(self):
        with self.nc.Block() as block:
            decos = {"tensor": block.tensor, "vector": block.vector,
                     "scalar": block.scalar, "sync": block.sync, "gpsimd": block.gpsimd}
            for name in ENGS:
                ops = self.ops[name]
                if ops:
                    def body(E, ops=tuple(ops)):
                        for f in ops:
                            f(E)
                    decos[name](body)
        self.ops = {e: [] for e in ENGS}


class Ring:

    def __init__(self, em, es, name, nslots, cols, dtype=BF16):
        nc = em.nc
        self.em = em
        self.nslots = nslots
        self.bufs = [es.enter_context(nc.sbuf_tensor(f"{name}{i}", [128, cols], dtype))
                     for i in range(nslots)]
        self.sems = [es.enter_context(nc.semaphore(f"s_{name}{i}")) for i in range(nslots)]
        self.loads = [0] * nslots
        self.free_at = [None] * nslots
        self.idx = 0

    def push(self, src_ap):
        s = self.idx % self.nslots
        self.idx += 1
        waits = [self.free_at[s]] if self.free_at[s] is not None else []
        self.loads[s] += 1
        tgt = self.loads[s] * 16
        self.em.op("gpsimd",
                   lambda g, b=self.bufs[s], src=src_ap: g.dma_start(b[:, :], src),
                   waits=waits, dma_inc=(self.sems[s], 16))
        return s, self.bufs[s], (self.sems[s], tgt)

    def consumed(self, slot, eng="tensor"):
        self.free_at[slot] = self.em.mark(eng)



def _f(a):
    return np.ascontiguousarray(np.asarray(a, dtype=np.float32))


def pack_shared(W):
    out = {}
    for nm, key in (("wq", "L0_attn_W_Q"), ("wk", "L0_attn_W_K"), ("wv", "L0_attn_W_V")):
        w = _f(W[key])
        out[nm] = np.ascontiguousarray(
            w.reshape(NH, DH, DC, 128).transpose(0, 3, 2, 1)).reshape(NH, 128, DC * DH)
    wo = _f(W["L0_attn_W_O"])
    out["wo"] = np.ascontiguousarray(
        wo.reshape(NH, DC, 128, DH).transpose(1, 3, 0, 2)).reshape(DC, 128, NH * 128)
    for nm, key in (("wg", "L0_ffn_w_gate"), ("wu", "L0_ffn_w_up")):
        w = _f(W[key])
        out[nm] = np.ascontiguousarray(
            w.reshape(NE, HC, 128, DC, 128).transpose(0, 1, 4, 3, 2)).reshape(NE, HC, 128, DC * 128)
    wd = _f(W["L0_ffn_w_down"])
    out["wd"] = np.ascontiguousarray(
        wd.reshape(NE, DC, 128, HC, 128).transpose(0, 3, 4, 1, 2)).reshape(NE, HC, 128, DC * 128)
    for nm, key in (("shg", "L0_ffn_sh_gate"), ("shu", "L0_ffn_sh_up")):
        w = _f(W[key])
        out[nm] = np.ascontiguousarray(
            w.reshape(16, 128, DC, 128).transpose(0, 3, 2, 1)).reshape(16, 128, DC * 128)
    shd = _f(W["L0_ffn_sh_down"])
    out["shd"] = np.ascontiguousarray(
        shd.reshape(DC, 128, 16, 128).transpose(2, 3, 0, 1)).reshape(16, 128, DC * 128)
    wm = _f(W["L0_wmod"])
    wa = wm.reshape(4, 4, 128, 12, 2, 512)
    wa = wa.transpose(3, 4, 0, 2, 1, 5)
    out["wmodA"] = np.ascontiguousarray(wa).reshape(96, 128, 4 * 512)
    gw = _f(W["L0_ffn_gate_w"])
    out["gatew"] = np.ascontiguousarray(
        gw.reshape(NE, DC, 128).transpose(2, 1, 0)).reshape(128, DC * NE)
    se = _f(W["L0_ffn_se_gate_w"])
    out["sew"] = np.ascontiguousarray(se.reshape(DC, 128).T)
    out["ident"] = np.eye(128, dtype=np.float32)
    tt = np.arange(128)
    out["btri"] = (tt[:, None] <= tt[None, :]).astype(np.float32)
    out["ones1"] = np.ones((128, 1), dtype=np.float32)
    out["onesr"] = np.ones((1, 128), dtype=np.float32)
    return out


def pack_core(x, c_all, core):
    b, par = core // 2, core % 2
    xb = _f(x[b])
    xg = np.ascontiguousarray(xb.reshape(TC, 128, DC, 128).transpose(0, 3, 2, 1))
    perm = [s ^ par for s in range(TC)]
    ximg = np.ascontiguousarray(xg[perm]).reshape(TC, 128, DC * 128)
    own = [2 * l + par for l in range(LC)]
    xo = np.ascontiguousarray(xg[own].transpose(1, 0, 2, 3)).reshape(128, LC * DC * 128)
    m2 = np.full((128, 128), float(par), dtype=np.float32)
    cc = np.ascontiguousarray(_f(c_all[b]).reshape(DC, 128).T)
    return {"ximg": ximg, "xo": xo, "m2": m2, "cimg": cc}


def unpack_out(res_list):
    out = np.empty((B, SEQ, D), dtype=np.float32)
    for core in range(NCORE):
        b, par = core // 2, core % 2
        img = np.asarray(res_list[core]["out"]).reshape(128, LC, DC, 128)
        piece = img.transpose(1, 3, 2, 0).reshape(LC, 128, D)
        for l in range(LC):
            g = 2 * l + par
            out[b, g * 128:(g + 1) * 128, :] = piece[l]
    return out



def build_nc():
    nc = bass.Bass("TRN2", target_bir_lowering=False)

    d_ximg = nc.dram_tensor("ximg", [TC, 128, DC * 128], F32, kind="ExternalInput")
    d_xo = nc.dram_tensor("xo", [128, LC * DC * 128], F32, kind="ExternalInput")
    d_m2 = nc.dram_tensor("m2", [128, 128], F32, kind="ExternalInput")
    d_cimg = nc.dram_tensor("cimg", [128, DC], F32, kind="ExternalInput")
    d_wq = nc.dram_tensor("wq", [NH, 128, DC * DH], F32, kind="ExternalInput")
    d_wk = nc.dram_tensor("wk", [NH, 128, DC * DH], F32, kind="ExternalInput")
    d_wv = nc.dram_tensor("wv", [NH, 128, DC * DH], F32, kind="ExternalInput")
    d_wo = nc.dram_tensor("wo", [DC, 128, NH * 128], F32, kind="ExternalInput")
    d_wg = nc.dram_tensor("wg", [NE, HC, 128, DC * 128], F32, kind="ExternalInput")
    d_wu = nc.dram_tensor("wu", [NE, HC, 128, DC * 128], F32, kind="ExternalInput")
    d_wd = nc.dram_tensor("wd", [NE, HC, 128, DC * 128], F32, kind="ExternalInput")
    d_shg = nc.dram_tensor("shg", [16, 128, DC * 128], F32, kind="ExternalInput")
    d_shu = nc.dram_tensor("shu", [16, 128, DC * 128], F32, kind="ExternalInput")
    d_shd = nc.dram_tensor("shd", [16, 128, DC * 128], F32, kind="ExternalInput")
    d_wmodA = nc.dram_tensor("wmodA", [96, 128, 2048], F32, kind="ExternalInput")
    d_gatew = nc.dram_tensor("gatew", [128, DC * NE], F32, kind="ExternalInput")
    d_sew = nc.dram_tensor("sew", [128, DC], F32, kind="ExternalInput")
    d_ident = nc.dram_tensor("ident", [128, 128], F32, kind="ExternalInput")
    d_btri = nc.dram_tensor("btri", [128, 128], F32, kind="ExternalInput")
    d_ones1 = nc.dram_tensor("ones1", [128, 1], F32, kind="ExternalInput")
    d_onesr = nc.dram_tensor("onesr", [1, 128], F32, kind="ExternalInput")
    d_out = nc.dram_tensor("out", [128, LC * DC * 128], F32, kind="ExternalOutput")

    es = ExitStack()
    esem = {e: es.enter_context(nc.semaphore(f"es_{e}")) for e in ENGS}
    em = Em(nc, esem)

    x_own = es.enter_context(nc.sbuf_tensor("x_own", [128, LC * DC * 128], F32))
    mod_sb = es.enter_context(nc.sbuf_tensor("mod_sb", [128, MC], F32))
    gacl = es.enter_context(nc.sbuf_tensor("gacl", [128, DC], F32))
    gbcl = es.enter_context(nc.sbuf_tensor("gbcl", [128, DC], F32))
    ca1 = es.enter_context(nc.sbuf_tensor("ca1", [128, DC], F32))
    cb1 = es.enter_context(nc.sbuf_tensor("cb1", [128, DC], F32))
    sgn_sc = es.enter_context(nc.sbuf_tensor("sgn_sc", [128, DC], F32))
    silu_c = es.enter_context(nc.sbuf_tensor("silu_c", [128, DC], BF16))
    c_f32 = es.enter_context(nc.sbuf_tensor("c_f32", [128, DC], F32))
    ident_bf = es.enter_context(nc.sbuf_tensor("ident_bf", [128, 128], BF16))
    ident_f = es.enter_context(nc.sbuf_tensor("ident_f", [128, 128], F32))
    btri_bf = es.enter_context(nc.sbuf_tensor("btri_bf", [128, 128], BF16))
    m2_bf = es.enter_context(nc.sbuf_tensor("m2_bf", [128, 128], BF16))
    ones_bf = es.enter_context(nc.sbuf_tensor("ones_bf", [128, 1], BF16))
    gatew_bf = es.enter_context(nc.sbuf_tensor("gatew_bf", [128, DC * NE], BF16))
    sew_bf = es.enter_context(nc.sbuf_tensor("sew_bf", [128, DC], BF16))
    ones_row = es.enter_context(nc.sbuf_tensor("ones_row", [1, 128], F32))
    ones_row_bf = es.enter_context(nc.sbuf_tensor("ones_row_bf", [1, 128], BF16))
    rowbf = es.enter_context(nc.sbuf_tensor("rowbf", [1, 512], BF16))
    rowbf2 = es.enter_context(nc.sbuf_tensor("rowbf2", [1, 512], BF16))
    rowa = es.enter_context(nc.sbuf_tensor("rowa", [1, 512], F32))
    rowb = es.enter_context(nc.sbuf_tensor("rowb", [1, 512], F32))
    ringA = Ring(em, es, "rA", 6, 1024, BF16)
    s_x = [es.enter_context(nc.semaphore(f"s_x{i}")) for i in range(2)]
    s_xo = es.enter_context(nc.semaphore("s_xo"))
    s_cf = es.enter_context(nc.semaphore("s_cf"))
    s_cg = es.enter_context(nc.semaphore("s_cg"))
    s_out = es.enter_context(nc.semaphore("s_out"))

    mod_evac = {}
    mod_state = {"t_prev": None, "s_prev": None, "row_prev": None}
    misc = {}

    def emit_mod_half(mb, half, ring, pma, pmt, stage=None):
        if stage == "b":
            sev2 = mod_state["sev_cur"]
            for j in range(4):
                mc = mb * 8 + half * 4 + j
                em.op("tensor",
                      lambda t, j=j: t.transpose(pmt, rowm[0:1, 128 * j:128 * j + 128],
                                                 ident_f[0:1, 0:1]),
                      waits=[sev2] if j == 0 else [mod_state["s_prev"]], inc=True)
                ttr = em.mark("tensor")
                mod_state["t_prev"] = ttr
                em.op("scalar", lambda s, mc=mc: s.copy(mod_sb[:, mc:mc + 1], pmt),
                      waits=[ttr], inc=True)
                mod_state["s_prev"] = em.mark("scalar")
                mod_evac[mc] = mod_state["s_prev"]
            return
        for dq in range(4):
            u = (mb * 2 + half) * 4 + dq
            sl0, buf0, ld0 = ring.push(d_wmodA[u, :, :])
            w0 = [ld0, misc["v_siluc"]]
            if dq == 0 and mod_state["row_prev"] is not None:
                w0.append(mod_state["row_prev"])
            for dcl in range(4):
                em.op("tensor",
                      lambda t, b=buf0, dcl=dcl, dq=dq: t.matmul(
                          pma, silu_c[:, dq * 4 + dcl:dq * 4 + dcl + 1],
                          b[:, dcl * 512:dcl * 512 + 512],
                          start=(dq == 0 and dcl == 0), stop=(dq == 3 and dcl == 3)),
                      waits=w0 if dcl == 0 else (), inc=(dcl == 3))
            ring.consumed(sl0, "tensor")
        tmm = em.mark("tensor")
        wv = [tmm]
        if mod_state["t_prev"] is not None:
            wv.append(mod_state["t_prev"])
        sev = em.op("scalar", lambda s: s.copy(rowm[0:1, 0:512], pma), waits=wv, inc=True)
        mod_state["row_prev"] = mod_state.get("row_cur")
        mod_state["row_cur"] = sev
        mod_state["sev_cur"] = sev
        if stage == "a":
            return
        sev2 = sev
        for j in range(4):
            mc = mb * 8 + half * 4 + j
            em.op("tensor",
                  lambda t, j=j: t.transpose(pmt, rowm[0:1, 128 * j:128 * j + 128],
                                             ident_f[0:1, 0:1]),
                  waits=[sev2] if j == 0 else [mod_state["s_prev"]], inc=True)
            ttr = em.mark("tensor")
            mod_state["t_prev"] = ttr
            em.op("scalar", lambda s, mc=mc: s.copy(mod_sb[:, mc:mc + 1], pmt),
                  waits=[ttr], inc=True)
            mod_state["s_prev"] = em.mark("scalar")
            mod_evac[mc] = mod_state["s_prev"]

    def emit_mod_unit(mc, ring, pma, pmt):
        sl0, buf0, ld0 = ring.push(d_wmodB[mc - 48, :, 0:1024])
        sl1, buf1, ld1 = ring.push(d_wmodB[mc - 48, :, 1024:2048])
        w0 = [ld0, ld1, misc["v_siluc"]]
        if mod_state["s_prev"] is not None:
            w0.append(mod_state["s_prev"])
        for dc in range(DC):
            b = buf0 if dc < 8 else buf1
            em.op("tensor",
                  lambda t, b=b, dc=dc: t.matmul(
                      pma, silu_c[:, dc:dc + 1], b[:, (dc % 8) * 128:(dc % 8) * 128 + 128],
                      start=(dc == 0), stop=(dc == DC - 1)),
                  waits=w0 if dc == 0 else (), inc=(dc == DC - 1))
        ring.consumed(sl0, "tensor")
        ring.consumed(sl1, "tensor")
        tmm = em.mark("tensor")
        wv = [tmm]
        if mod_state["t_prev"] is not None:
            wv.append(mod_state["t_prev"])
        sev = em.op("scalar", lambda s: s.copy(rowb[0:1, 384:512], pma), waits=wv, inc=True)
        em.op("tensor", lambda t: t.transpose(pmt, rowb[0:1, 384:512],
                                              ident_f[0:1, 0:1]),
              waits=[sev], inc=True)
        ttr = em.mark("tensor")
        mod_state["t_prev"] = ttr
        em.op("scalar", lambda s, mc=mc: s.copy(mod_sb[:, mc:mc + 1], pmt),
              waits=[ttr], inc=True)
        mod_state["s_prev"] = em.mark("scalar")
        mod_evac[mc] = mod_state["s_prev"]

    def emit_clamp(col0, dst):
        w = [mod_evac[col0 + DC - 1]]
        s_abs = em.op("scalar", lambda s: s.activation(dst[:, :], mod_sb[:, col0:col0 + DC],
                                                       AF.Abs), waits=w, inc=True)
        v_max = em.op("vector", lambda v: v.tensor_scalar_max(dst[:, :], dst[:, :], 0.25),
                      waits=[s_abs], inc=True)
        em.drain("vector")
        s_sgn = em.op("scalar", lambda s: s.activation(sgn_sc[:, :], mod_sb[:, col0:col0 + DC],
                                                       AF.Sign), inc=True)
        r = em.op("vector", lambda v: v.tensor_mul(dst[:, :], dst[:, :], sgn_sc[:, :]),
                  waits=[s_sgn, v_max], inc=True)
        em.drain("vector")
        return r

    def emit_bcast(row_ap, dst_sb_ap, psum_ap, waits=(), bf=False):
        ones = ones_row_bf if bf else ones_row
        wl = list(waits) + [misc["CF" if not bf else "CG"]]
        t_b = em.op("tensor",
                    lambda t: t.matmul(psum_ap, ones[:, :], row_ap,
                                       start=True, stop=True),
                    waits=wl, inc=True)
        s_b = em.op("scalar", lambda s: s.copy(dst_sb_ap, psum_ap), waits=[t_b], inc=True)
        return s_b

    def emit_ln_cast(x_src_ap, xbfb, xsqb, state, extra_waits=()):
        w_cast = list(extra_waits)
        if state.get("stats_prev"):
            w_cast.append(state["stats_prev"])
        em.op("scalar", lambda s: s.activation(xbfb[:, :], x_src_ap, AF.Copy),
              waits=w_cast, inc=True)
        s_c2 = em.op("scalar", lambda s: s.activation(xsqb[:, :], x_src_ap, AF.Square),
                     inc=True)
        state["cast_mark"] = s_c2

    def emit_ln_piece(x_src_ap, dst_fn, cvec, svec_col, prow_s, prow_q, state, rows_b, pbc, t1b, xbfb, xsqb, fill_fn=None, fill_fn2=None, precast_fn=None):
        w_mm = [state["cast_mark"], misc["CG"]]
        if state.get("rows_read"):
            w_mm.append(state["rows_read"])
        for dc in range(DC):
            em.op("tensor",
                  lambda t, dc=dc: t.matmul(prow_s, ones_bf[:, :],
                                            xbfb[:, dc * 128:(dc + 1) * 128],
                                            start=(dc == 0), stop=(dc == DC - 1)),
                  waits=w_mm if dc == 0 else (), inc=(dc == DC - 1))
            em.op("tensor",
                  lambda t, dc=dc: t.matmul(prow_q, ones_bf[:, :],
                                            xsqb[:, dc * 128:(dc + 1) * 128],
                                            start=(dc == 0), stop=(dc == DC - 1)),
                  inc=(dc == DC - 1))
        t_stats = em.mark("tensor")
        state["stats_pp"] = state.get("stats_prev")
        state["stats_prev"] = t_stats
        w_row = [t_stats]
        if state.get("rows_sc_prev"):
            w_row.append(state["rows_sc_prev"])
        if state.get("bc_read"):
            w_row.append(state["bc_read"])
        em.op("scalar", lambda s: s.mul(rowa[0:1, 0:128], prow_s, 1.0 / D),
              waits=w_row, inc=True)
        em.op("scalar", lambda s: s.mul(rowa[0:1, 128:256], prow_q, 1.0 / D))
        em.drain("scalar")
        s_m2 = em.op("scalar", lambda s: s.activation(rowb[0:1, 0:128], rowa[0:1, 0:128],
                                                      AF.Square), inc=True)
        state["rows_read"] = s_m2
        v_var = em.op("vector", lambda v: v.tensor_sub(rowb[0:1, 128:256], rowa[0:1, 128:256],
                                                       rowb[0:1, 0:128]),
                      waits=[s_m2] + ([state["v_prev"]] if state.get("v_prev") else []),
                      inc=True)
        em.drain("vector")
        em.op("vector", lambda v: v.tensor_scalar_add(rowb[0:1, 128:256],
                                                      rowb[0:1, 128:256], 1e-5), inc=True)
        v_vare = em.mark("vector")
        s_sd = em.op("scalar", lambda s: s.activation(rowb[0:1, 256:384], rowb[0:1, 128:256],
                                                      AF.Sqrt),
                     waits=[v_vare], inc=True)
        v_rs = em.op("vector", lambda v: v.reciprocal(rowa[0:1, 256:384], rowb[0:1, 256:384]),
                     waits=[s_sd, state.get("bc_read")], inc=True)
        em.drain("vector")
        v_bp = em.op("vector", lambda v: v.tensor_mul(rowa[0:1, 128:256], rowa[0:1, 0:128],
                                                      rowa[0:1, 256:384]),
                     waits=[state.get("bc_read")], inc=True)
        em.drain("vector")
        state["rows_sc_prev"] = v_bp
        w_bc = [v_bp]
        if state.get("bc_prev"):
            w_bc.append(state["bc_prev"])
        if state.get("bcp_prev"):
            w_bc.append(state["bcp_prev"])
        if state.get("rowbf_free"):
            w_bc.append(state["rowbf_free"])
        s_cv = em.op("scalar", lambda s: s.copy(rowbf[:, :], rowa[0:1, 0:512]),
                     waits=w_bc, inc=True)
        if precast_fn is not None:
            precast_fn()
        if fill_fn is not None:
            fill_fn()
        g_bc = emit_bcast(rowbf[:, :], rows_b[:, :], pbc, waits=[s_cv], bf=True)
        state["rowbf_free"] = g_bc
        state["bc_read"] = g_bc
        state["bcp_prev"] = g_bc
        if fill_fn2 is not None:
            fill_fn2()
        a3 = bass.AP(rows_b, 256, [[512, 128], [0, DC], [1, 128]])
        b3 = bass.AP(rows_b, 128, [[512, 128], [0, DC], [1, 128]])
        x3 = x_src_ap.rearrange("p (c t) -> p c t", c=DC)
        t3 = t1b[:, :].rearrange("p (c t) -> p c t", c=DC)
        em.op("vector", lambda v: v.tensor_tensor(t3, x3, a3, ALU.mult),
              waits=[g_bc], inc=True)
        em.drain("vector")
        em.op("vector", lambda v: v.tensor_tensor(t3, t3, b3, ALU.subtract), inc=True)
        em.drain("vector")
        w_ts = [state["mod_wait"]] if state.get("mod_wait") else []
        for dc in range(DC):
            em.op("vector",
                  lambda v, dc=dc: v.tensor_scalar(
                      dst_fn(dc), t1b[:, dc * 128:(dc + 1) * 128],
                      cvec[:, dc:dc + 1], mod_sb[:, svec_col + dc:svec_col + dc + 1],
                      ALU.mult, ALU.add),
                  waits=w_ts if dc == 0 else (), inc=(dc == DC - 1))
        v_h = em.mark("vector")
        state["v_prev"] = v_h
        state["bc_prev"] = v_h
        state["mod_wait"] = None
        return v_h

    es_h = ExitStack()
    h_full = es_h.enter_context(nc.sbuf_tensor("h_full", [128, DC * 2048], BF16))

    es1 = ExitStack()
    xl = [es1.enter_context(nc.sbuf_tensor(f"xl{i}", [128, DC * 128], F32)) for i in range(2)]
    xbf = es1.enter_context(nc.sbuf_tensor("xbf", [128, DC * 128], BF16))
    xsq = es1.enter_context(nc.sbuf_tensor("xsq", [128, DC * 128], BF16))
    rows_b1 = es1.enter_context(nc.sbuf_tensor("rows_b1", [128, 512], BF16))
    t1sb = es1.enter_context(nc.sbuf_tensor("t1sb", [128, DC * 128], F32))
    p_rs = es1.enter_context(nc.psum_tensor("p_rs", [1, 128], F32))
    p_rq = es1.enter_context(nc.psum_tensor("p_rq", [1, 128], F32))
    p_ma = es1.enter_context(nc.psum_tensor("p_ma", [1, 512], F32))
    p_ma2 = es1.enter_context(nc.psum_tensor("p_ma2", [1, 512], F32))
    p_mt = es1.enter_context(nc.psum_tensor("p_mt", [128, 1], F32))
    p_bc1 = es1.enter_context(nc.psum_tensor("p_bc1", [128, 512], F32))
    ringB = Ring(em, es1, "rB", 4, 2048, BF16)
    rowm = es1.enter_context(nc.sbuf_tensor("rowm", [1, 512], F32))

    ncg = 0
    for buf, src in ((ident_bf, d_ident), (btri_bf, d_btri), (m2_bf, d_m2),
                     (ones_bf, d_ones1), (gatew_bf, d_gatew), (sew_bf, d_sew),
                     (ones_row_bf, d_onesr)):
        em.op("gpsimd", lambda g, b=buf, s=src: g.dma_start(b[:, :], s[:, :]),
              dma_inc=(s_cg, 16))
        ncg += 16
    misc["CG"] = (s_cg, ncg)
    misc["CF"] = (s_cf, 48)
    for buf, src in ((ident_f, d_ident), (c_f32, d_cimg), (ones_row, d_onesr)):
        em.op("sync", lambda sy, b=buf, s=src: sy.dma_start(b[:, :], s[:, :]),
              dma_inc=(s_cf, 16))
    em.op("vector", lambda v: v.memset(rowa[0:1, 384:512], 0.0), inc=True)
    em.drain("vector")
    misc["v_siluc"] = em.op("scalar", lambda s: s.activation(silu_c[:, :], c_f32[:, :],
                                                             AF.Silu),
                            waits=[(s_cf, 48)], inc=True)
    em.op("sync", lambda sy: sy.dma_start(x_own[:, :], d_xo[:, :]), dma_inc=(s_xo, 16))

    if SUB1 >= 1:
        for mbh in range(8):
            emit_mod_half(mbh // 2, mbh % 2,
                          ringB, (p_ma if mbh % 2 == 0 else p_ma2)[0:1, 0:512],
                          p_mt[:, :])
        v_ca1 = em.op("vector", lambda v: v.tensor_scalar_add(ca1[:, :], mod_sb[:, 16:32],
                                                              1.0),
                      waits=[mod_evac[31]], inc=True)
        em.drain("vector")

    ln1_state = {"mod_wait": v_ca1 if SUB1 >= 1 else None}
    x_loads = [0, 0]
    x_free = [None, None]
    def ld_x(s_idx):
        slot = (s_idx // 2) % 2
        w_ld = [x_free[slot]] if x_free[slot] is not None else []
        x_loads[slot] += 1
        em.op("sync", lambda sy, sl=slot, s=s_idx: sy.dma_start(xl[sl][:, :], d_ximg[s, :, :]),
              waits=w_ld, dma_inc=(s_x[slot], 16))
        return (s_x[slot], x_loads[slot] * 16)

    x_wait = [None] * TC

    def src_of(s):
        if s % 2 == 0:
            return x_own[:, (s // 2) * 2048:(s // 2 + 1) * 2048]
        return xl[(s // 2) % 2][:, :]

    if SUB1 >= 2:
        x_wait[1] = ld_x(1)
        emit_ln_cast(src_of(0), xbf[:, :], xsq[:, :], ln1_state,
                     extra_waits=[(s_xo, 16)])
    for s_idx in (range(TC) if SUB1 >= 2 else range(0)):
        if s_idx % 2 == 1 and s_idx + 2 < TC:
            x_wait[s_idx + 2] = ld_x(s_idx + 2)
        pc = None
        if s_idx + 1 < TC:
            nxt = s_idx + 1
            ew = [(s_xo, 16)] if nxt % 2 == 0 else [x_wait[nxt]]
            pc = (lambda nxt=nxt, ew=ew:
                  emit_ln_cast(src_of(nxt), xbf[:, :], xsq[:, :],
                               ln1_state, extra_waits=ew))

        def dst(dc, s_idx=s_idx):
            off = dc * 2048 + s_idx * 128
            return h_full[:, off:off + 128]
        fa = fb = None
        if SUB1 >= 1:
            mbh = 8 + s_idx
            pma_h = (p_ma if s_idx % 2 == 0 else p_ma2)[0:1, 0:512]
            fa = (lambda mbh=mbh, pma_h=pma_h:
                  emit_mod_half(mbh // 2, mbh % 2, ringB, pma_h, p_mt[:, :], stage="a"))
            fb = (lambda mbh=mbh, pma_h=pma_h:
                  emit_mod_half(mbh // 2, mbh % 2, ringB, pma_h, p_mt[:, :], stage="b"))
        v_h = emit_ln_piece(src_of(s_idx), dst, ca1, 0, p_rs[0:1, 0:128],
                            p_rq[0:1, 0:128], ln1_state, rows_b1, p_bc1[:, :],
                            t1sb, xbf, xsq, fill_fn=fa, fill_fn2=fb,
                            precast_fn=pc)
        if s_idx % 2 == 1:
            x_free[(s_idx // 2) % 2] = v_h
    if SUB1 >= 1 and PHASES >= 2:
        emit_clamp(32, gacl)

    em.run_block()
    es1.close()

    if PHASES < 2:
        em.op("sync", lambda sy: sy.dma_start(d_out[:, :], x_own[:, :]),
              waits=[(s_xo, 16)], dma_inc=(s_out, 16))
        em.op("sync", lambda sy: sy.wait_ge(s_out, 16))
        em.run_block()
        es_h.close()
        es.close()
        return nc

    es2 = ExitStack()
    z_all = es2.enter_context(nc.sbuf_tensor("z_all", [128, NH * 1024], BF16))
    k_sb = es2.enter_context(nc.sbuf_tensor("k_sb", [128, 2048], BF16))
    vfm = es2.enter_context(nc.sbuf_tensor("vfm", [128, 2048], BF16))
    vtok = es2.enter_context(nc.sbuf_tensor("vtok", [128, 2048], BF16))
    q_sb = es2.enter_context(nc.sbuf_tensor("q_sb", [128, 1024], BF16))
    pt_sb = [es2.enter_context(nc.sbuf_tensor(f"pt{i}", [128, 1024], BF16)) for i in range(4)]
    negm_b = es2.enter_context(nc.sbuf_tensor("negm_b", [128, 512], BF16))
    rl_b = es2.enter_context(nc.sbuf_tensor("rl_b", [128, 512], BF16))
    m4 = es2.enter_context(nc.sbuf_tensor("m4", [128, 8], F32))
    mtmp = es2.enter_context(nc.sbuf_tensor("mtmp", [128, 1], F32))
    pacc = es2.enter_context(nc.psum_tensor("pacc", [128, 1024], F32))
    pstp = es2.enter_context(nc.psum_tensor("pstp", [128, 1024], F32))
    plr = es2.enter_context(nc.psum_tensor("plr", [1, 512], F32))
    pz = es2.enter_context(nc.psum_tensor("pz", [128, 512], F32))
    ptv = [es2.enter_context(nc.psum_tensor(f"ptv{i}", [128, 128], BF16)) for i in range(2)]

    g = {"pacc": [None, None], "pst": [None, None], "ptv": [None, None],
         "plr": None, "pz": None, "negm": None, "rl": None, "rowa2": None,
         "k": None, "v": None, "vt": None, "q": None, "pp": [None, None],
         "ptsb": [None, None, None, None], "zread": None}

    def proj_quarters(dst_sb, w_units, n_q, moving_fn, evac_scale, prev_use, qtr_base=0):
        last = None
        for qtr0 in range(n_q):
            qtr = qtr_base + qtr0
            bank = qtr % 2
            w0 = [w_units[0][1], w_units[1][1]]
            if g["pacc"][bank] is not None:
                w0.append(g["pacc"][bank])
            for dc in range(DC):
                ubuf = w_units[dc // 8][0]
                em.op("tensor",
                      lambda t, ub=ubuf, dc=dc, qtr=qtr, bank=bank:
                      t.matmul(pacc[:, bank * 512:bank * 512 + 512],
                               ub[:, (dc % 8) * 128:(dc % 8) * 128 + 128],
                               moving_fn(dc, qtr),
                               start=(dc == 0), stop=(dc == DC - 1)),
                      waits=w0 if dc == 0 else (), inc=(dc == DC - 1))
            tm = em.mark("tensor")
            w_ev = [tm]
            if qtr0 < 2 and prev_use is not None:
                w_ev.append(prev_use)
            ev = em.op("scalar",
                       lambda s, qtr=qtr, bank=bank: s.activation(
                           dst_sb[:, qtr * 512:qtr * 512 + 512],
                           pacc[:, bank * 512:bank * 512 + 512],
                           AF.Copy, scale=evac_scale),
                       waits=w_ev, inc=True)
            g["pacc"][bank] = ev
            last = ev
        return last

    stats_rot = [0]
    stat_guard = [None, None]

    def emit_stats_job(job, grp, kq_done, q_done):
        (li, j) = job
        l = grp * 4 + li
        ncols = (2 * l + 2) * 128
        c0 = j * 512
        cw = min(512, ncols - c0)
        bank = stats_rot[0] % 2
        stats_rot[0] += 1
        w0 = [q_done, kq_done]
        if stat_guard[bank] is not None:
            w0.append(stat_guard[bank])
        if stats_rot[0] <= 2:
            w0.append(g["pp"][0])
            w0.append(g["pp"][1])
        em.op("tensor",
              lambda t, l=l, c0=c0, cw=cw, bank=bank: t.matmul(
                  pstp[:, bank * 512:bank * 512 + cw],
                  q_sb[:, l * 128:l * 128 + 128],
                  k_sb[:, c0:c0 + cw], start=True, stop=True),
              waits=w0, inc=True)
        tm = em.mark("tensor")
        col = l
        if j == 0:
            rv = em.op("vector",
                       lambda v, col=col, cw=cw, bank=bank: v.reduce_max(
                           m4[:, col:col + 1], pstp[:, bank * 512:bank * 512 + cw],
                           axis=AX.X),
                       waits=[tm, g.get("m4col")], inc=True)
        else:
            em.op("vector",
                  lambda v, cw=cw, bank=bank: v.reduce_max(
                      mtmp[:, :], pstp[:, bank * 512:bank * 512 + cw], axis=AX.X),
                  waits=[tm], inc=True)
            em.drain("vector")
            rv = em.op("vector",
                       lambda v, col=col: v.tensor_max(m4[:, col:col + 1],
                                                       m4[:, col:col + 1], mtmp[:, :]),
                       inc=True)
        em.drain("vector")
        stat_guard[bank] = rv
        return rv

    def stats_jobs_for(grp):
        jobs = []
        for li in range(4):
            l = grp * 4 + li
            nj = ((2 * l + 2) * 128 + 511) // 512
            for j in range(nj):
                jobs.append((li, j))
        return jobs

    pair_rot = [0]
    pt_rot = [0]

    def emit_pair_front(h, grp, R, p, negm_row, negm_done, vtok_done):
        tca, tcb = 2 * p, 2 * p + 1
        pr = pair_rot[0] % 2
        pti = pt_rot[0] % 4
        pair_rot[0] += 1
        pt_rot[0] += 1
        ptens = pstp if pr == 0 else pacc
        ptb = pt_sb[pti]
        def q0_of(tc):
            li = max(0, tc // 2 - 4 * grp)
            return li * 128
        q0a, q0b = q0_of(tca), q0_of(tcb)
        w_st = [negm_done, misc["CG"]]
        if g["pp"][pr] is not None:
            w_st.append(g["pp"][pr])
        if p == 0:
            w_st += [stat_guard[0], stat_guard[1]]
        if p <= 1 and pr == 1:
            w_st += [g["pacc"][0], g["pacc"][1]]
        for k, q0 in ((0, q0a), (1, q0b)):
            em.op("tensor",
                  lambda t, k=k, q0=q0, ptens=ptens, negm_row=negm_row: t.matmul(
                      ptens[:, k * 512 + q0:k * 512 + 512], ones_row_bf[:, :],
                      negm_row[0:1, q0:512], start=True, stop=False),
                  waits=w_st if k == 0 else (), inc=False)
        for k, tc, q0 in ((0, tca, q0a), (1, tcb, q0b)):
            em.op("tensor",
                  lambda t, tc=tc, k=k, q0=q0, ptens=ptens, grp=grp: t.matmul(
                      ptens[:, k * 512 + q0:k * 512 + 512],
                      k_sb[:, tc * 128:tc * 128 + 128],
                      q_sb[:, grp * 512 + q0:grp * 512 + 512], start=False, stop=True),
                  waits=(), inc=(k == 1))
        t_st = em.mark("tensor")
        w_exp = [t_st]
        if g["ptsb"][pti] is not None:
            w_exp.append(g["ptsb"][pti])
        em.op("scalar",
              lambda s, ptb=ptb, ptens=ptens, q0=q0a: s.activation(
                  ptb[:, q0:512], ptens[:, q0:512], AF.Exp),
              waits=w_exp, inc=(q0b >= 512))
        if q0b < 512:
            em.op("scalar",
                  lambda s, ptb=ptb, ptens=ptens, q0=q0b: s.activation(
                      ptb[:, 512 + q0:1024], ptens[:, 512 + q0:1024], AF.Exp),
                  inc=True)
        s_exp = em.mark("scalar")
        g["pp"][pr] = s_exp
        if pr == 1:
            g["pacc"][0] = s_exp
            g["pacc"][1] = s_exp
        last_dep = s_exp
        for k, tc in ((0, tca), (1, tcb)):
            for li in range(4):
                l = grp * 4 + li
                c0 = k * 512 + li * 128
                if tc == 2 * l:
                    last_dep = em.op("vector",
                                     lambda v, ptb=ptb, c0=c0: v.tensor_mul(
                                         ptb[:, c0:c0 + 128], ptb[:, c0:c0 + 128],
                                         btri_bf[:, :]),
                                     waits=[s_exp], inc=True)
                    em.drain("vector")
                elif tc == 2 * l + 1:
                    last_dep = em.op("vector",
                                     lambda v, ptb=ptb, c0=c0: v.tensor_mul(
                                         ptb[:, c0:c0 + 128], ptb[:, c0:c0 + 128],
                                         m2_bf[:, :]),
                                     waits=[s_exp], inc=True)
                    em.drain("vector")
        return (p, pti, ptb, last_dep, tca, tcb, q0a, q0b)

    def emit_pair_back(ctx, R, vtok_done):
        (p, pti, ptb, last_dep, tca, tcb, q0a, q0b) = ctx
        grp_off = {}
        w_l = [last_dep]
        if p == 0:
            if g["plr"] is not None:
                w_l.append(g["plr"])
        first = True
        for k, q0 in ((0, q0a), (1, q0b)):
            if q0 >= 512:
                continue
            em.op("tensor",
                  lambda t, ptb=ptb, k=k, q0=q0, p=p, R=R: t.matmul(
                      plr[0:1, q0:512], ones_bf[:, :],
                      ptb[:, k * 512 + q0:k * 512 + 512],
                      start=(p == 0 and k == 0), stop=(p == R // 2 - 1 and k == 1)),
                  waits=w_l if first else (), inc=(k == 1))
            first = False
        w_av = []
        if p == 0:
            w_av.append(vtok_done)
            if g["pz"] is not None:
                w_av.append(g["pz"])
        first = True
        for k, tc, q0 in ((0, tca, q0a), (1, tcb, q0b)):
            if q0 >= 512:
                continue
            em.op("tensor",
                  lambda t, ptb=ptb, k=k, tc=tc, q0=q0, p=p, R=R: t.matmul(
                      pz[:, q0:512], vtok[:, tc * 128:tc * 128 + 128],
                      ptb[:, k * 512 + q0:k * 512 + 512],
                      start=(p == 0 and k == 0), stop=(p == R // 2 - 1 and k == 1)),
                  waits=w_av if first else (), inc=(k == 1))
            first = False
        t_av = em.mark("tensor")
        g["ptsb"][pti] = t_av
        return t_av

    for h in range(NH):
        units = {}
        for nm, dten in (("wk", d_wk), ("wq", d_wq), ("wv", d_wv)):
            us = []
            for half in range(2):
                slot, buf, ld = ringA.push(dten[h, :, half * 1024:half * 1024 + 1024])
                us.append([buf, ld, slot])
            units[nm] = us
        kq_done = proj_quarters(
            k_sb, units["wk"], 4,
            lambda dc, qtr: h_full[:, dc * 2048 + qtr * 512: dc * 2048 + qtr * 512 + 512],
            1.0, g["k"])
        for u in units["wk"]:
            ringA.consumed(u[2])
        q_done = proj_quarters(
            q_sb, units["wq"], 2,
            lambda dc, qh: bass.AP(h_full, dc * 2048 + qh * 1024,
                                   [[DC * 2048, 128], [256, 4], [1, 128]]),
            SCALE, g["q"])
        for u in units["wq"]:
            ringA.consumed(u[2])
        stats_rot[0] = 0
        all_jobs = [(0, j) for j in stats_jobs_for(0)] + [(1, j) for j in stats_jobs_for(1)]
        jit = iter(all_jobs)
        stat_red = {}

        def pop_job():
            it = next(jit, None)
            if it is not None:
                grp_, job_ = it
                stat_red[grp_] = emit_stats_job(job_, grp_, kq_done, q_done)

        vq_done = None
        for qtr in range(4):
            pop_job()
            pop_job()
            vq_done = proj_quarters(
                vfm, units["wv"], 1,
                lambda dc, q: h_full[:, dc * 2048 + q * 512: dc * 2048 + q * 512 + 512],
                1.0, g["v"] if qtr == 0 else None, qtr_base=qtr)
        for u in units["wv"]:
            ringA.consumed(u[2])
        vt_last = None
        for tc in range(TC):
            pop_job()
            bk = tc % 2
            w_tr = [vq_done] if tc < 2 else []
            if g["ptv"][bk] is not None:
                w_tr.append(g["ptv"][bk])
            em.op("tensor",
                  lambda t, tc=tc, bk=bk: t.transpose(
                      ptv[bk][:, :], vfm[:, tc * 128:tc * 128 + 128], ident_bf[:, :]),
                  waits=w_tr, inc=True)
            ttr = em.mark("tensor")
            w_ev = [ttr]
            if tc < 2 and g["vt"] is not None:
                w_ev.append(g["vt"])
            ev = em.op("scalar",
                       lambda s, tc=tc, bk=bk: s.copy(vtok[:, tc * 128:tc * 128 + 128],
                                                      ptv[bk][:, :]),
                       waits=w_ev, inc=True)
            g["ptv"][bk] = ev
            vt_last = ev
        for grp_, job_ in jit:
            stat_red[grp_] = emit_stats_job(job_, grp_, kq_done, q_done)

        pend_q = []

        def grp_tail(grp, t_av_last, nrow):
            w_rc = [t_av_last]
            if g["rowa2"] is not None:
                w_rc.append(g["rowa2"])

            def _recip(v, nrow=nrow):
                with nc.allow_low_precision(reason="1/l multiplier applied to bf16 z"):
                    return v.reciprocal(nrow[:, :], plr[0:1, 0:512])
            v_rec = em.op("vector", _recip,
                          waits=w_rc + [g["ngrow" + str(grp)]], inc=True)
            em.drain("vector")
            g["plr"] = v_rec
            bk = grp
            w_rl = [v_rec, g["pacc"][bk]]
            if g["rl"] is not None:
                w_rl.append(g["rl"])
            if g["pp"][1] is not None:
                w_rl.append(g["pp"][1])
            g_rl = emit_bcast(nrow[:, :], rl_b[:, :],
                              pacc[:, bk * 512:bk * 512 + 512], waits=w_rl, bf=True)
            g["pp"][1] = g_rl
            g["pacc"][bk] = g_rl
            g["rowa2"] = g_rl
            g["ngrow" + str(grp)] = g_rl
            v_z = em.op("vector",
                        lambda v, h=h, grp=grp: v.tensor_mul(
                            z_all[:, h * 1024 + grp * 512: h * 1024 + grp * 512 + 512],
                            pz[:, :], rl_b[:, :]),
                        waits=[t_av_last, g_rl], inc=True)
            em.drain("vector")
            g["pz"] = v_z
            g["rl"] = v_z
            g["zlast"] = v_z

        tail_due = []

        def pop_back():
            (grp0, R0, nrow0, ctx) = pend_q.pop(0)
            t_av = emit_pair_back(ctx, R0, vt_last)
            if not any(e[0] == grp0 for e in pend_q):
                g["ngrow" + str(grp0)] = em.mark("tensor")
                tail_due.append((grp0, t_av, nrow0))

        def flush_tails():
            while tail_due:
                (grp0, t_av, nrow0) = tail_due.pop(0)
                grp_tail(grp0, t_av, nrow0)

        for grp in range(2):
            R = 8 * grp + 8
            nrow = rowbf if grp == 0 else rowbf2
            last_red = stat_red[grp]
            pair_rot[0] = 0
            w_tr = [last_red, stat_guard[0]]
            if g["pp"][0] is not None:
                w_tr.append(g["pp"][0])
            for li in range(4):
                em.op("tensor",
                      lambda t, li=li, grp=grp: t.transpose(
                          pstp[0:1, li * 128:li * 128 + 128],
                          m4[:, grp * 4 + li:grp * 4 + li + 1], ident_f[:, :]),
                      waits=w_tr if li == 0 else (), inc=(li == 3))
            t_mr = em.mark("tensor")
            g["m4col"] = t_mr
            w_ng = [t_mr]
            if g.get("ngrow" + str(grp)) is not None:
                w_ng.append(g["ngrow" + str(grp)])
            s_ng = em.op("scalar", lambda s, nrow=nrow: s.mul(nrow[:, :],
                                                             pstp[0:1, 0:512], -1.0),
                         waits=w_ng, inc=True)
            g["pp"][0] = s_ng
            for p in range(R // 2):
                ctx = emit_pair_front(h, grp, R, p, nrow, s_ng, vt_last)
                pend_q.append((grp, R, nrow, ctx))
                flush_tails()
                while len(pend_q) > 3:
                    pop_back()
        while pend_q:
            pop_back()
        flush_tails()
        g["k"] = em.mark("tensor")
        g["v"] = em.mark("tensor")
        g["vt"] = em.mark("tensor")
        g["q"] = em.mark("tensor")

    zs_done = g["zlast"]
    op_guard = [None, None, None, None]
    first_doc_wait = [(s_xo, 16), zs_done]
    for doc in range(DC):
        slot, buf, ld = ringA.push(d_wo[doc, :, 0:1024])
        slot2, buf2, ld2 = ringA.push(d_wo[doc, :, 1024:2048])
        pair = doc % 2
        for qh in range(2):
            bi = pair * 2 + qh
            tgt = (pacc[:, qh * 512:qh * 512 + 512] if pair == 0
                   else pstp[:, qh * 512:qh * 512 + 512])
            w0 = [ld, ld2] + ([op_guard[bi]] if op_guard[bi] is not None else [])
            if doc < 2:
                w0 += first_doc_wait + [g["pacc"][0], g["pacc"][1],
                                        g["pp"][0], g["pp"][1]]
            for hh in range(NH):
                ub = buf if hh < 8 else buf2
                em.op("tensor",
                      lambda t, ub=ub, hh=hh, qh=qh, tgt=tgt: t.matmul(
                          tgt, ub[:, (hh % 8) * 128:(hh % 8) * 128 + 128],
                          z_all[:, hh * 1024 + qh * 512: hh * 1024 + qh * 512 + 512],
                          start=(hh == 0), stop=(hh == NH - 1)),
                      waits=w0 if hh == 0 else (), inc=(hh == NH - 1))
            tm = em.mark("tensor")
            x_ap = bass.AP(x_own, (4 * qh) * 2048 + doc * 128,
                           [[LC * 2048, 128], [2048, 4], [1, 128]])
            tgt3 = tgt.rearrange("p (a b) -> p a b", a=4)
            ev = em.op("vector",
                       lambda v, tgt3=tgt3, x_ap=x_ap, doc=doc: v.scalar_tensor_tensor(
                           x_ap, tgt3, gacl[:, doc:doc + 1], x_ap, ALU.mult, ALU.add),
                       waits=[tm], inc=True)
            em.drain("vector")
            op_guard[bi] = ev
        ringA.consumed(slot)
        ringA.consumed(slot2)
    x_attn_done = em.mark("vector")

    if PHASES < 3:
        em.op("sync", lambda sy: sy.dma_start(d_out[:, :], x_own[:, :]),
              waits=[x_attn_done, (s_xo, 16)], dma_inc=(s_out, 16))
        em.op("sync", lambda sy: sy.wait_ge(s_out, 16))
        em.run_block()
        es2.close()
        es_h.close()
        es.close()
        return nc

    em.run_block()
    es2.close()
    es_h.close()

    es3 = ExitStack()
    h2 = es3.enter_context(nc.sbuf_tensor("h2", [128, DC * 1024], BF16))
    x1s = es3.enter_context(nc.sbuf_tensor("x1s", [128, HC * 1024], BF16))
    ysb = es3.enter_context(nc.sbuf_tensor("ysb", [128, HC * 1024], BF16))
    xbf2 = es3.enter_context(nc.sbuf_tensor("xbf2", [128, DC * 128], BF16))
    xsq2 = es3.enter_context(nc.sbuf_tensor("xsq2", [128, DC * 128], BF16))
    t1f = es3.enter_context(nc.sbuf_tensor("t1f", [128, DC * 128], F32))
    rows_b2 = es3.enter_context(nc.sbuf_tensor("rows_b2", [128, 512], BF16))
    w3c = es3.enter_context(nc.sbuf_tensor("w3c", [128, 1], F32))
    w4c = es3.enter_context(nc.sbuf_tensor("w4c", [128, 1], F32))
    comb_set = [[es3.enter_context(nc.sbuf_tensor(f"comb_{s}_{i}", [128, 512], BF16))
                 for i in range(2)] for s in range(2)]
    gst = es3.enter_context(nc.sbuf_tensor("gst", [128, LC * NE], F32))
    grow = es3.enter_context(nc.sbuf_tensor("grow", [8, 1024], F32))
    gtmp = es3.enter_context(nc.sbuf_tensor("gtmp", [128, NE], F32))
    w1c = es3.enter_context(nc.sbuf_tensor("w1c", [128, 1], F32))
    w2c = es3.enter_context(nc.sbuf_tensor("w2c", [128, 1], F32))
    eq2 = es3.enter_context(nc.sbuf_tensor("eq2", [128, NE], F32))
    rowc0 = es3.enter_context(nc.sbuf_tensor("rowc0", [1, 512], BF16))
    rowc1 = es3.enter_context(nc.sbuf_tensor("rowc1", [1, 512], BF16))
    rowd0 = es3.enter_context(nc.sbuf_tensor("rowd0", [1, 512], BF16))
    rowd1 = es3.enter_context(nc.sbuf_tensor("rowd1", [1, 512], BF16))
    s_rd = es3.enter_context(nc.semaphore("s_rd"))
    ringD = Ring(em, es3, "rD", 18, 512, BF16)
    px1 = es3.enter_context(nc.psum_tensor("px1", [128, 1024], F32))
    px3 = es3.enter_context(nc.psum_tensor("px3", [128, 1024], F32))
    pdn = [es3.enter_context(nc.psum_tensor(f"pdn{i}", [128, 512], F32)) for i in range(3)]
    pm2 = es3.enter_context(nc.psum_tensor("pm2", [128, 512], F32))

    v_cb1 = em.op("vector", lambda v: v.tensor_scalar_add(cb1[:, :], mod_sb[:, 64:80], 1.0),
                  waits=[mod_evac[79]], inc=True)
    em.drain("vector")
    emit_clamp(80, gbcl)
    v_gbcl = [em.mark("vector")]

    ln2_state = {"mod_wait": v_cb1}
    emit_ln_cast(x_own[:, 0:2048], xbf2[:, :], xsq2[:, :], ln2_state)
    for lc in range(LC):
        src = x_own[:, lc * 2048:(lc + 1) * 2048]
        pc2 = None
        if lc + 1 < LC:
            pc2 = (lambda lc=lc: emit_ln_cast(
                x_own[:, (lc + 1) * 2048:(lc + 2) * 2048],
                xbf2[:, :], xsq2[:, :], ln2_state))

        def dst2(dc, lc=lc):
            off = dc * 1024 + lc * 128
            return h2[:, off:off + 128]
        emit_ln_piece(src, dst2, cb1, 48, pdn[0][0:1, 0:128], pdn[1][0:1, 0:128],
                      ln2_state, rows_b2, pdn[2][:, :], t1f, xbf2, xsq2,
                      precast_fn=pc2)
    h2_done = em.mark("vector")

    for half in range(2):
        for dc in range(DC):
            em.op("tensor",
                  lambda t, dc=dc, half=half: t.matmul(
                      px1[0:8, half * 512:half * 512 + 512],
                      gatew_bf[:, dc * 8:dc * 8 + 8],
                      h2[:, dc * 1024 + half * 512: dc * 1024 + half * 512 + 512],
                      start=(dc == 0), stop=(dc == DC - 1)),
                  waits=[h2_done, misc["CG"]] if dc == 0 else (), inc=(dc == DC - 1))
    t_gsc = em.mark("tensor")
    s_gev = em.op("scalar", lambda s: s.copy(grow[:, :], px1[0:8, 0:1024]),
                  waits=[t_gsc], inc=True)
    prev_cp = s_gev
    for lc in range(LC):
        em.op("tensor",
              lambda t, lc=lc: t.transpose(px3[:, 0:8],
                                           grow[0:8, lc * 128:lc * 128 + 128],
                                           ident_f[0:8, 0:8]),
              waits=[prev_cp], inc=True)
        t_gt = em.mark("tensor")
        prev_cp = em.op("scalar",
                        lambda s, lc=lc: s.copy(gst[:, lc * 8:lc * 8 + 8], px3[:, 0:8]),
                        waits=[t_gt], inc=True)
    s_gst = em.mark("scalar")

    comb_rows_done = None
    prev_tct = None
    for lc in range(LC):
        sl = gst[:, lc * 8:lc * 8 + 8]
        em.op("vector", lambda v, sl=sl: v.reduce_max(w1c[:, :], sl, axis=AX.X),
              waits=[s_gst] if lc == 0 else (), inc=True)
        em.drain("vector")
        em.op("vector", lambda v, sl=sl: v.tensor_scalar(gtmp[:, :], sl, w1c[:, :], None,
                                                         ALU.is_equal),
              waits=[prev_tct], inc=True)
        em.drain("vector")
        em.op("vector", lambda v, sl=sl: v.scalar_tensor_tensor(eq2[:, :], gtmp[:, :],
                                                                -1.0e30, sl, ALU.mult,
                                                                ALU.add), inc=True)
        em.drain("vector")
        em.op("vector", lambda v: v.reduce_max(w2c[:, :], eq2[:, :], axis=AX.X), inc=True)
        em.drain("vector")
        v_d = em.op("vector", lambda v: v.tensor_sub(w3c[:, :], w1c[:, :], w2c[:, :]),
                    waits=[prev_tct], inc=True)
        em.drain("vector")
        s_w1 = em.op("scalar", lambda s: s.activation(w4c[:, :], w3c[:, :], AF.Sigmoid),
                     waits=[v_d, prev_tct], inc=True)
        em.op("vector", lambda v: v.tensor_scalar(eq2[:, :], eq2[:, :], w2c[:, :], None,
                                                  ALU.is_equal), inc=True)
        em.drain("vector")
        v_w2 = em.op("vector", lambda v: v.tensor_scalar(w3c[:, :], w4c[:, :], -1.0,
                                                         1.0, ALU.mult, ALU.add),
                     waits=[s_w1], inc=True)
        em.drain("vector")
        em.op("vector", lambda v: v.tensor_scalar(gtmp[:, :], gtmp[:, :], w4c[:, :], None,
                                                  ALU.mult), waits=[s_w1], inc=True)
        em.drain("vector")
        em.op("vector", lambda v: v.tensor_scalar(eq2[:, :], eq2[:, :], w3c[:, :], None,
                                                  ALU.mult), waits=[v_w2], inc=True)
        em.drain("vector")
        v_cb = em.op("vector", lambda v: v.tensor_add(gtmp[:, :], gtmp[:, :], eq2[:, :]),
                     inc=True)
        em.drain("vector")
        em.op("tensor",
              lambda t, lc=lc: t.transpose(px3[0:8, 0:128], gtmp[:, :], ident_f[:, :]),
              waits=[v_cb, prev_cp], inc=True)
        t_ct = em.mark("tensor")
        prev_tct = t_ct
        comb_rows_done = em.op("scalar",
                               lambda s, lc=lc: s.copy(grow[:, lc * 128:lc * 128 + 128],
                                                       px3[0:8, 0:128]),
                               waits=[t_ct], inc=True)
        prev_cp = comb_rows_done
    for half in range(2):
        for dc in range(DC):
            em.op("tensor",
                  lambda t, dc=dc, half=half: t.matmul(
                      px1[0:1, half * 512:half * 512 + 512],
                      sew_bf[:, dc:dc + 1],
                      h2[:, dc * 1024 + half * 512: dc * 1024 + half * 512 + 512],
                      start=(dc == 0), stop=(dc == DC - 1)),
                  waits=[s_gev] if (dc == 0 and half == 0) else (), inc=(dc == DC - 1))
    t_se = em.mark("tensor")
    em.op("scalar", lambda s: s.activation(rowc0[:, :], px1[0:1, 0:512],
                                           AF.Sigmoid), waits=[t_se], inc=True)
    s_sg = em.op("scalar", lambda s: s.activation(rowc1[:, :], px1[0:1, 512:1024],
                                                  AF.Sigmoid), inc=True)

    def push_gu(e2, hc2):
        gu = []
        for proj in range(2):
            us = []
            for half in range(2):
                if e2 >= NE:
                    dten = d_shg if proj == 0 else d_shu
                    srcp = dten[(e2 - NE) * 8 + hc2, :, half * 1024:half * 1024 + 1024]
                else:
                    dten = d_wg if proj == 0 else d_wu
                    srcp = dten[e2, hc2, :, half * 1024:half * 1024 + 1024]
                us.append(ringA.push(srcp))
            gu.append(us)
        return gu

    gu_pend = [push_gu(0, 0)]
    dpre_next = {0: None}
    ln2_rows_done = em.mark("scalar")
    misc["pm2_free"] = em.mark("scalar")
    gx = {"px1": s_sg, "px3": comb_rows_done, "comb_ready": {},
          "pdn": [ln2_rows_done, ln2_rows_done, ln2_rows_done],
          "x1s": None, "ysb": None, "comb": None}
    y_marks = {}
    pdn_rot = [0]
    for e in range(NEU):
        shared = e >= NE
        cset = 0 if e == 9 else e % 2
        if e == 0:
            w_cb = [comb_rows_done, s_sg, gx["px1"]]
            misc["n_rd"] = misc.get("n_rd", 0) + 32
            em.op("gpsimd", lambda gy: gy.dma_start(rowd0[:, :], grow[0:1, 0:512]),
                  waits=w_cb, dma_inc=(s_rd, 16))
            em.op("gpsimd", lambda gy: gy.dma_start(rowd1[:, :], grow[0:1, 512:1024]),
                  dma_inc=(s_rd, 16))
            b0 = emit_bcast(rowd0[:, :], comb_set[0][0][:, :], px1[:, 0:512],
                            waits=[(s_rd, misc["n_rd"])], bf=True)
            g_cb = emit_bcast(rowd1[:, :], comb_set[0][1][:, :], px1[:, 0:512],
                              waits=[b0], bf=True)
            gx["px1"] = g_cb
        else:
            g_cb = gx["comb_ready"][cset]
        def push_down(q, e=e, shared=shared):
            us = []
            for hc in range(HC):
                if shared:
                    srcap = d_shd[(e - NE) * 8 + hc, :, q * 512:q * 512 + 512]
                else:
                    srcap = d_wd[e, hc, :, q * 512:q * 512 + 512]
                us.append(ringD.push(srcap))
            return us
        dpre = {}
        if dpre_next.get(0) is not None:
            dpre[0] = dpre_next[0]
            dpre_next[0] = None
        for hc in range(HC):
            gu = gu_pend[0]
            if hc == HC - 2 and dpre.get(0) is None:
                dpre[0] = push_down(0)
            w1_ = [gu[0][0][2], gu[0][1][2], h2_done]
            if gx["px1"] is not None:
                w1_.append(gx["px1"])
            for dc in range(DC):
                ub = gu[0][dc // 8][1]
                for half in range(2):
                    em.op("tensor",
                          lambda t, ub=ub, dc=dc, half=half: t.matmul(
                              px1[:, half * 512:half * 512 + 512],
                              ub[:, (dc % 8) * 128:(dc % 8) * 128 + 128],
                              h2[:, dc * 1024 + half * 512: dc * 1024 + half * 512 + 512],
                              start=(dc == 0), stop=(dc == DC - 1)),
                          waits=w1_ if (dc == 0 and half == 0) else (),
                          inc=(dc == DC - 1 and half == 1))
            t_x1 = em.mark("tensor")
            ringA.consumed(gu[0][0][0])
            ringA.consumed(gu[0][1][0])
            w_si = [t_x1]
            if y_marks.get(hc) is not None:
                w_si.append(y_marks[hc])
            s_si = em.op("scalar",
                         lambda s, hc=hc: s.activation(x1s[:, hc * 1024:hc * 1024 + 1024],
                                                       px1[:, :], AF.Silu),
                         waits=w_si, inc=True)
            gx["px1"] = s_si
            w3_ = [gu[1][0][2], gu[1][1][2]]
            if gx["px3"] is not None:
                w3_.append(gx["px3"])
            for dc in range(DC):
                ub = gu[1][dc // 8][1]
                for half in range(2):
                    em.op("tensor",
                          lambda t, ub=ub, dc=dc, half=half: t.matmul(
                              px3[:, half * 512:half * 512 + 512],
                              ub[:, (dc % 8) * 128:(dc % 8) * 128 + 128],
                              h2[:, dc * 1024 + half * 512: dc * 1024 + half * 512 + 512],
                              start=(dc == 0), stop=(dc == DC - 1)),
                          waits=w3_ if (dc == 0 and half == 0) else (),
                          inc=(dc == DC - 1 and half == 1))
            t_x3 = em.mark("tensor")
            ringA.consumed(gu[1][0][0])
            ringA.consumed(gu[1][1][0])
            nxt = (e, hc + 1) if hc < HC - 1 else (e + 1, 0)
            if nxt[0] < NEU:
                gu_pend[0] = push_gu(nxt[0], nxt[1])
            w_y = [t_x3, s_si, g_cb]
            if gx["ysb"] is not None and hc == 0:
                w_y.append(gx["ysb"])
            em.op("vector",
                  lambda v, hc=hc: v.tensor_mul(ysb[:, hc * 1024:hc * 1024 + 1024],
                                                x1s[:, hc * 1024:hc * 1024 + 1024],
                                                px3[:, :]),
                  waits=w_y, inc=True)
            em.drain("vector")
            em.op("vector",
                  lambda v, hc=hc, cset=cset: v.tensor_mul(
                      ysb[:, hc * 1024:hc * 1024 + 512],
                      ysb[:, hc * 1024:hc * 1024 + 512],
                      comb_set[cset][0][:, :]), inc=True)
            em.drain("vector")
            v_y = em.op("vector",
                        lambda v, hc=hc, cset=cset: v.tensor_mul(
                            ysb[:, hc * 1024 + 512:hc * 1024 + 1024],
                            ysb[:, hc * 1024 + 512:hc * 1024 + 1024],
                            comb_set[cset][1][:, :]), inc=True)
            em.drain("vector")
            gx[("comb_used", cset)] = v_y
            gx["px3"] = v_y
            y_marks[hc] = v_y
        gx["comb"] = em.mark("vector")
        y_done = em.mark("vector")
        en = e + 1
        if en <= NE:
            nset = en % 2
            w_cb = [gx.get(("comb_used", nset))]
            if misc.get("g_cb_prev") is not None:
                w_cb.append(misc["g_cb_prev"])
            w_cb.append(misc["pm2_free"])
            if en < NE:
                misc["n_rd"] = misc.get("n_rd", 0) + 32
                em.op("gpsimd",
                      lambda gy, en=en: gy.dma_start(rowd0[:, :], grow[en:en + 1, 0:512]),
                      waits=w_cb + [comb_rows_done], dma_inc=(s_rd, 16))
                em.op("gpsimd",
                      lambda gy, en=en: gy.dma_start(rowd1[:, :],
                                                     grow[en:en + 1, 512:1024]),
                      dma_inc=(s_rd, 16))
                b0 = emit_bcast(rowd0[:, :], comb_set[nset][0][:, :], pm2[:, 0:512],
                                waits=[(s_rd, misc["n_rd"])], bf=True)
                g_nc = emit_bcast(rowd1[:, :], comb_set[nset][1][:, :], pm2[:, 0:512],
                                  waits=[b0], bf=True)
            else:
                b0 = emit_bcast(rowc0[:, :], comb_set[nset][0][:, :], pm2[:, 0:512],
                                waits=w_cb + [s_sg], bf=True)
                g_nc = emit_bcast(rowc1[:, :], comb_set[nset][1][:, :], pm2[:, 0:512],
                                  waits=[b0], bf=True)
            misc["g_cb_prev"] = g_nc
            gx["comb_ready"][nset] = g_nc
        for q in range(4):
            if q < 3:
                dpre[q + 1] = push_down(q + 1)
            elif e + 1 < NEU:
                dpre[0] = None
            dunits = dpre[q]
            for dloc in range(4):
                doc = q * 4 + dloc
                for th in range(2):
                    rot = pdn_rot[0] % 3
                    pdn_rot[0] += 1
                    wd0 = [u[2] for u in dunits] + [y_done]
                    if gx["pdn"][rot] is not None:
                        wd0.append(gx["pdn"][rot])
                    if e == 0 and doc == 0 and th == 0:
                        wd0 += [x_attn_done, (s_xo, 16), v_gbcl[0]]
                    for hc in range(HC):
                        em.op("tensor",
                              lambda t, hc=hc, dloc=dloc, th=th, rot=rot, ub=dunits[hc][1]:
                              t.matmul(pdn[rot][:, :],
                                       ub[:, dloc * 128:dloc * 128 + 128],
                                       ysb[:, hc * 1024 + th * 512: hc * 1024 + th * 512 + 512],
                                       start=(hc == 0), stop=(hc == HC - 1)),
                              waits=wd0 if hc == 0 else (), inc=(hc == HC - 1))
                    tdm = em.mark("tensor")
                    x_ap = bass.AP(x_own, (4 * th) * 2048 + doc * 128,
                                   [[LC * 2048, 128], [2048, 4], [1, 128]])
                    p3 = pdn[rot][:, :].rearrange("p (a b) -> p a b", a=4)
                    evd = em.op("vector",
                                lambda v, p3=p3, x_ap=x_ap, doc=doc: v.scalar_tensor_tensor(
                                    x_ap, p3, gbcl[:, doc:doc + 1], x_ap, ALU.mult, ALU.add),
                                waits=[tdm], inc=True)
                    em.drain("vector")
                    gx["pdn"][rot] = evd
            for u in dunits:
                ringD.consumed(u[0])
            if q == 3 and e + 1 < NEU:
                def push_down_next(e2=e + 1):
                    us = []
                    for hc in range(HC):
                        if e2 >= NE:
                            srcap = d_shd[(e2 - NE) * 8 + hc, :, 0:512]
                        else:
                            srcap = d_wd[e2, hc, :, 0:512]
                        us.append(ringD.push(srcap))
                    return us
                dpre_next[0] = push_down_next()
        gx["x1s"] = em.mark("vector")
        gx["x1s"] = em.mark("vector")
        gx["ysb"] = em.mark("tensor")

    x_done = em.mark("vector")
    em.op("sync", lambda sy: sy.dma_start(d_out[:, :], x_own[:, :]),
          waits=[x_done], dma_inc=(s_out, 16))
    em.op("sync", lambda sy: sy.wait_ge(s_out, 16))

    em.run_block()
    es3.close()
    es.close()
    return nc


def kernel(**inputs):
    shared = pack_shared(inputs)
    x = np.asarray(inputs["L0_attn_x"])
    c_all = np.asarray(inputs["L0_c"])
    in_maps = []
    for core in range(NCORE):
        m = dict(shared)
        m.update(pack_core(x, c_all, core))
        in_maps.append(m)
    nc = build_nc()
    res = run_bass_kernel_spmd(nc, in_maps, core_ids=list(range(NCORE)))
    return unpack_out(res.results)


# version 19
# speedup vs baseline: 1.2130x; 1.2130x over previous
import numpy as np
from contextlib import ExitStack

import concourse.bass as bass
import concourse.mybir as mybir
from concourse.bass_utils import run_bass_kernel_spmd

F32 = mybir.dt.float32
BF16 = mybir.dt.bfloat16
AF = mybir.ActivationFunctionType
ALU = mybir.AluOpType
AX = mybir.AxisListType

D = 2048
DC = 16
NH = 16
DH = 128
SEQ = 2048
TC = 16
LC = 8
B = 4
NCORE = 8
NE = 8
NEU = 10
HC = 8
MC = 96
SCALE = float(1.0 / np.sqrt(128.0))

ENGS = ("tensor", "vector", "scalar", "sync", "gpsimd")
PHASES = 3
SUB1 = 2


class Em:

    def __init__(self, nc, esem):
        self.nc = nc
        self.esem = esem
        self.cnt = {e: 0 for e in ENGS}
        self.ops = {e: [] for e in ENGS}

    def op(self, eng, fn, waits=(), inc=False, dma_inc=None):
        if inc:
            self.cnt[eng] += 1
        sem_h = self.esem[eng]
        wl = tuple(w for w in waits if w is not None)

        def emit(E, fn=fn, wl=wl, inc=inc, sem_h=sem_h, dma_inc=dma_inc):
            for (s, v) in wl:
                E.wait_ge(s, v)
            ins = fn(E)
            if inc:
                ins.then_inc(sem_h, 1)
            elif dma_inc is not None:
                ins.then_inc(dma_inc[0], dma_inc[1])

        self.ops[eng].append(emit)
        return (sem_h, self.cnt[eng])

    def drain(self, eng):
        self.ops[eng].append(lambda E: E.drain())

    def mark(self, eng):
        return (self.esem[eng], self.cnt[eng])

    def run_block(self):
        with self.nc.Block() as block:
            decos = {"tensor": block.tensor, "vector": block.vector,
                     "scalar": block.scalar, "sync": block.sync, "gpsimd": block.gpsimd}
            for name in ENGS:
                ops = self.ops[name]
                if ops:
                    def body(E, ops=tuple(ops)):
                        for f in ops:
                            f(E)
                    decos[name](body)
        self.ops = {e: [] for e in ENGS}


class Ring:

    def __init__(self, em, es, name, nslots, cols, dtype=BF16):
        nc = em.nc
        self.em = em
        self.nslots = nslots
        self.bufs = [es.enter_context(nc.sbuf_tensor(f"{name}{i}", [128, cols], dtype))
                     for i in range(nslots)]
        self.sems = [es.enter_context(nc.semaphore(f"s_{name}{i}")) for i in range(nslots)]
        self.loads = [0] * nslots
        self.free_at = [None] * nslots
        self.idx = 0

    def push(self, src_ap):
        s = self.idx % self.nslots
        self.idx += 1
        waits = [self.free_at[s]] if self.free_at[s] is not None else []
        self.loads[s] += 1
        tgt = self.loads[s] * 16
        self.em.op("gpsimd",
                   lambda g, b=self.bufs[s], src=src_ap: g.dma_start(b[:, :], src),
                   waits=waits, dma_inc=(self.sems[s], 16))
        return s, self.bufs[s], (self.sems[s], tgt)

    def consumed(self, slot, eng="tensor"):
        self.free_at[slot] = self.em.mark(eng)



def _f(a):
    return np.ascontiguousarray(np.asarray(a, dtype=np.float32))


def pack_shared(W):
    out = {}
    for nm, key in (("wq", "L0_attn_W_Q"), ("wk", "L0_attn_W_K"), ("wv", "L0_attn_W_V")):
        w = _f(W[key])
        out[nm] = np.ascontiguousarray(
            w.reshape(NH, DH, DC, 128).transpose(0, 3, 2, 1)).reshape(NH, 128, DC * DH)
    wo = _f(W["L0_attn_W_O"])
    out["wo"] = np.ascontiguousarray(
        wo.reshape(NH, DC, 128, DH).transpose(1, 3, 0, 2)).reshape(DC, 128, NH * 128)
    for nm, key in (("wg", "L0_ffn_w_gate"), ("wu", "L0_ffn_w_up")):
        w = _f(W[key])
        out[nm] = np.ascontiguousarray(
            w.reshape(NE, HC, 128, DC, 128).transpose(0, 1, 4, 3, 2)).reshape(NE, HC, 128, DC * 128)
    wd = _f(W["L0_ffn_w_down"])
    out["wd"] = np.ascontiguousarray(
        wd.reshape(NE, DC, 128, HC, 128).transpose(0, 3, 4, 1, 2)).reshape(NE, HC, 128, DC * 128)
    for nm, key in (("shg", "L0_ffn_sh_gate"), ("shu", "L0_ffn_sh_up")):
        w = _f(W[key])
        out[nm] = np.ascontiguousarray(
            w.reshape(16, 128, DC, 128).transpose(0, 3, 2, 1)).reshape(16, 128, DC * 128)
    shd = _f(W["L0_ffn_sh_down"])
    out["shd"] = np.ascontiguousarray(
        shd.reshape(DC, 128, 16, 128).transpose(2, 3, 0, 1)).reshape(16, 128, DC * 128)
    wm = _f(W["L0_wmod"])
    wa = wm.reshape(4, 4, 128, 12, 2, 512)
    wa = wa.transpose(3, 4, 0, 2, 1, 5)
    out["wmodA"] = np.ascontiguousarray(wa).reshape(96, 128, 4 * 512)
    gw = _f(W["L0_ffn_gate_w"])
    out["gatew"] = np.ascontiguousarray(
        gw.reshape(NE, DC, 128).transpose(2, 1, 0)).reshape(128, DC * NE)
    se = _f(W["L0_ffn_se_gate_w"])
    out["sew"] = np.ascontiguousarray(se.reshape(DC, 128).T)
    out["ident"] = np.eye(128, dtype=np.float32)
    tt = np.arange(128)
    out["btri"] = (tt[:, None] <= tt[None, :]).astype(np.float32)
    out["ones1"] = np.ones((128, 1), dtype=np.float32)
    out["onesr"] = np.ones((1, 128), dtype=np.float32)
    return out


def pack_core(x, c_all, core):
    b, par = core // 2, core % 2
    xb = _f(x[b])
    xg = np.ascontiguousarray(xb.reshape(TC, 128, DC, 128).transpose(0, 3, 2, 1))
    perm = [s ^ par for s in range(TC)]
    ximg = np.ascontiguousarray(xg[perm]).reshape(TC, 128, DC * 128)
    own = [2 * l + par for l in range(LC)]
    xo = np.ascontiguousarray(xg[own].transpose(1, 0, 2, 3)).reshape(128, LC * DC * 128)
    m2 = np.full((128, 128), float(par), dtype=np.float32)
    cc = np.ascontiguousarray(_f(c_all[b]).reshape(DC, 128).T)
    return {"ximg": ximg, "xo": xo, "m2": m2, "cimg": cc}


def unpack_out(res_list):
    out = np.empty((B, SEQ, D), dtype=np.float32)
    for core in range(NCORE):
        b, par = core // 2, core % 2
        img = np.asarray(res_list[core]["out"]).reshape(128, LC, DC, 128)
        piece = img.transpose(1, 3, 2, 0).reshape(LC, 128, D)
        for l in range(LC):
            g = 2 * l + par
            out[b, g * 128:(g + 1) * 128, :] = piece[l]
    return out



def build_nc():
    nc = bass.Bass("TRN2", target_bir_lowering=False)

    d_ximg = nc.dram_tensor("ximg", [TC, 128, DC * 128], F32, kind="ExternalInput")
    d_xo = nc.dram_tensor("xo", [128, LC * DC * 128], F32, kind="ExternalInput")
    d_m2 = nc.dram_tensor("m2", [128, 128], F32, kind="ExternalInput")
    d_cimg = nc.dram_tensor("cimg", [128, DC], F32, kind="ExternalInput")
    d_wq = nc.dram_tensor("wq", [NH, 128, DC * DH], F32, kind="ExternalInput")
    d_wk = nc.dram_tensor("wk", [NH, 128, DC * DH], F32, kind="ExternalInput")
    d_wv = nc.dram_tensor("wv", [NH, 128, DC * DH], F32, kind="ExternalInput")
    d_wo = nc.dram_tensor("wo", [DC, 128, NH * 128], F32, kind="ExternalInput")
    d_wg = nc.dram_tensor("wg", [NE, HC, 128, DC * 128], F32, kind="ExternalInput")
    d_wu = nc.dram_tensor("wu", [NE, HC, 128, DC * 128], F32, kind="ExternalInput")
    d_wd = nc.dram_tensor("wd", [NE, HC, 128, DC * 128], F32, kind="ExternalInput")
    d_shg = nc.dram_tensor("shg", [16, 128, DC * 128], F32, kind="ExternalInput")
    d_shu = nc.dram_tensor("shu", [16, 128, DC * 128], F32, kind="ExternalInput")
    d_shd = nc.dram_tensor("shd", [16, 128, DC * 128], F32, kind="ExternalInput")
    d_wmodA = nc.dram_tensor("wmodA", [96, 128, 2048], F32, kind="ExternalInput")
    d_gatew = nc.dram_tensor("gatew", [128, DC * NE], F32, kind="ExternalInput")
    d_sew = nc.dram_tensor("sew", [128, DC], F32, kind="ExternalInput")
    d_ident = nc.dram_tensor("ident", [128, 128], F32, kind="ExternalInput")
    d_btri = nc.dram_tensor("btri", [128, 128], F32, kind="ExternalInput")
    d_ones1 = nc.dram_tensor("ones1", [128, 1], F32, kind="ExternalInput")
    d_onesr = nc.dram_tensor("onesr", [1, 128], F32, kind="ExternalInput")
    d_out = nc.dram_tensor("out", [128, LC * DC * 128], F32, kind="ExternalOutput")

    es = ExitStack()
    esem = {e: es.enter_context(nc.semaphore(f"es_{e}")) for e in ENGS}
    em = Em(nc, esem)

    x_own = es.enter_context(nc.sbuf_tensor("x_own", [128, LC * DC * 128], F32))
    mod_sb = es.enter_context(nc.sbuf_tensor("mod_sb", [128, MC], F32))
    gacl = es.enter_context(nc.sbuf_tensor("gacl", [128, DC], F32))
    gbcl = es.enter_context(nc.sbuf_tensor("gbcl", [128, DC], F32))
    ca1 = es.enter_context(nc.sbuf_tensor("ca1", [128, DC], F32))
    cb1 = es.enter_context(nc.sbuf_tensor("cb1", [128, DC], F32))
    sgn_sc = es.enter_context(nc.sbuf_tensor("sgn_sc", [128, DC], F32))
    silu_c = es.enter_context(nc.sbuf_tensor("silu_c", [128, DC], BF16))
    c_f32 = es.enter_context(nc.sbuf_tensor("c_f32", [128, DC], F32))
    ident_bf = es.enter_context(nc.sbuf_tensor("ident_bf", [128, 128], BF16))
    ident_f = es.enter_context(nc.sbuf_tensor("ident_f", [128, 128], F32))
    btri_bf = es.enter_context(nc.sbuf_tensor("btri_bf", [128, 128], BF16))
    m2_bf = es.enter_context(nc.sbuf_tensor("m2_bf", [128, 128], BF16))
    ones_bf = es.enter_context(nc.sbuf_tensor("ones_bf", [128, 1], BF16))
    gatew_bf = es.enter_context(nc.sbuf_tensor("gatew_bf", [128, DC * NE], BF16))
    sew_bf = es.enter_context(nc.sbuf_tensor("sew_bf", [128, DC], BF16))
    ones_row = es.enter_context(nc.sbuf_tensor("ones_row", [1, 128], F32))
    ones_row_bf = es.enter_context(nc.sbuf_tensor("ones_row_bf", [1, 128], BF16))
    rowbf = es.enter_context(nc.sbuf_tensor("rowbf", [1, 512], BF16))
    rowbf2 = es.enter_context(nc.sbuf_tensor("rowbf2", [1, 512], BF16))
    rowa = es.enter_context(nc.sbuf_tensor("rowa", [1, 512], F32))
    rowb = es.enter_context(nc.sbuf_tensor("rowb", [1, 512], F32))
    ringA = Ring(em, es, "rA", 6, 1024, BF16)
    s_x = [es.enter_context(nc.semaphore(f"s_x{i}")) for i in range(2)]
    s_xo = es.enter_context(nc.semaphore("s_xo"))
    s_cf = es.enter_context(nc.semaphore("s_cf"))
    s_cg = es.enter_context(nc.semaphore("s_cg"))
    s_out = es.enter_context(nc.semaphore("s_out"))

    mod_evac = {}
    mod_state = {"t_prev": None, "s_prev": None, "row_prev": None}
    misc = {}

    def emit_mod_half(mb, half, ring, pma, pmt, stage=None):
        if stage == "b":
            sev2 = mod_state["sev_cur"]
            for j in range(4):
                mc = mb * 8 + half * 4 + j
                em.op("tensor",
                      lambda t, j=j: t.transpose(pmt, rowm[0:1, 128 * j:128 * j + 128],
                                                 ident_f[0:1, 0:1]),
                      waits=[sev2] if j == 0 else [mod_state["s_prev"]], inc=True)
                ttr = em.mark("tensor")
                mod_state["t_prev"] = ttr
                em.op("scalar", lambda s, mc=mc: s.copy(mod_sb[:, mc:mc + 1], pmt),
                      waits=[ttr], inc=True)
                mod_state["s_prev"] = em.mark("scalar")
                mod_evac[mc] = mod_state["s_prev"]
            return
        for dq in range(4):
            u = (mb * 2 + half) * 4 + dq
            sl0, buf0, ld0 = ring.push(d_wmodA[u, :, :])
            w0 = [ld0, misc["v_siluc"]]
            if dq == 0 and mod_state["row_prev"] is not None:
                w0.append(mod_state["row_prev"])
            for dcl in range(4):
                em.op("tensor",
                      lambda t, b=buf0, dcl=dcl, dq=dq: t.matmul(
                          pma, silu_c[:, dq * 4 + dcl:dq * 4 + dcl + 1],
                          b[:, dcl * 512:dcl * 512 + 512],
                          start=(dq == 0 and dcl == 0), stop=(dq == 3 and dcl == 3)),
                      waits=w0 if dcl == 0 else (), inc=(dcl == 3))
            ring.consumed(sl0, "tensor")
        tmm = em.mark("tensor")
        wv = [tmm]
        if mod_state["t_prev"] is not None:
            wv.append(mod_state["t_prev"])
        sev = em.op("scalar", lambda s: s.copy(rowm[0:1, 0:512], pma), waits=wv, inc=True)
        mod_state["row_prev"] = mod_state.get("row_cur")
        mod_state["row_cur"] = sev
        mod_state["sev_cur"] = sev
        if stage == "a":
            return
        sev2 = sev
        for j in range(4):
            mc = mb * 8 + half * 4 + j
            em.op("tensor",
                  lambda t, j=j: t.transpose(pmt, rowm[0:1, 128 * j:128 * j + 128],
                                             ident_f[0:1, 0:1]),
                  waits=[sev2] if j == 0 else [mod_state["s_prev"]], inc=True)
            ttr = em.mark("tensor")
            mod_state["t_prev"] = ttr
            em.op("scalar", lambda s, mc=mc: s.copy(mod_sb[:, mc:mc + 1], pmt),
                  waits=[ttr], inc=True)
            mod_state["s_prev"] = em.mark("scalar")
            mod_evac[mc] = mod_state["s_prev"]

    def emit_mod_unit(mc, ring, pma, pmt):
        sl0, buf0, ld0 = ring.push(d_wmodB[mc - 48, :, 0:1024])
        sl1, buf1, ld1 = ring.push(d_wmodB[mc - 48, :, 1024:2048])
        w0 = [ld0, ld1, misc["v_siluc"]]
        if mod_state["s_prev"] is not None:
            w0.append(mod_state["s_prev"])
        for dc in range(DC):
            b = buf0 if dc < 8 else buf1
            em.op("tensor",
                  lambda t, b=b, dc=dc: t.matmul(
                      pma, silu_c[:, dc:dc + 1], b[:, (dc % 8) * 128:(dc % 8) * 128 + 128],
                      start=(dc == 0), stop=(dc == DC - 1)),
                  waits=w0 if dc == 0 else (), inc=(dc == DC - 1))
        ring.consumed(sl0, "tensor")
        ring.consumed(sl1, "tensor")
        tmm = em.mark("tensor")
        wv = [tmm]
        if mod_state["t_prev"] is not None:
            wv.append(mod_state["t_prev"])
        sev = em.op("scalar", lambda s: s.copy(rowb[0:1, 384:512], pma), waits=wv, inc=True)
        em.op("tensor", lambda t: t.transpose(pmt, rowb[0:1, 384:512],
                                              ident_f[0:1, 0:1]),
              waits=[sev], inc=True)
        ttr = em.mark("tensor")
        mod_state["t_prev"] = ttr
        em.op("scalar", lambda s, mc=mc: s.copy(mod_sb[:, mc:mc + 1], pmt),
              waits=[ttr], inc=True)
        mod_state["s_prev"] = em.mark("scalar")
        mod_evac[mc] = mod_state["s_prev"]

    def emit_clamp(col0, dst):
        w = [mod_evac[col0 + DC - 1]]
        s_abs = em.op("scalar", lambda s: s.activation(dst[:, :], mod_sb[:, col0:col0 + DC],
                                                       AF.Abs), waits=w, inc=True)
        v_max = em.op("vector", lambda v: v.tensor_scalar_max(dst[:, :], dst[:, :], 0.25),
                      waits=[s_abs], inc=True)
        em.drain("vector")
        s_sgn = em.op("scalar", lambda s: s.activation(sgn_sc[:, :], mod_sb[:, col0:col0 + DC],
                                                       AF.Sign), inc=True)
        r = em.op("vector", lambda v: v.tensor_mul(dst[:, :], dst[:, :], sgn_sc[:, :]),
                  waits=[s_sgn, v_max], inc=True)
        em.drain("vector")
        return r

    def emit_bcast(row_ap, dst_sb_ap, psum_ap, waits=(), bf=False):
        ones = ones_row_bf if bf else ones_row
        wl = list(waits) + [misc["CF" if not bf else "CG"]]
        t_b = em.op("tensor",
                    lambda t: t.matmul(psum_ap, ones[:, :], row_ap,
                                       start=True, stop=True),
                    waits=wl, inc=True)
        s_b = em.op("scalar", lambda s: s.copy(dst_sb_ap, psum_ap), waits=[t_b], inc=True)
        return s_b

    def emit_ln_cast(x_src_ap, xbfb, xsqb, state, extra_waits=()):
        w_cast = list(extra_waits)
        if state.get("stats_prev"):
            w_cast.append(state["stats_prev"])
        em.op("scalar", lambda s: s.activation(xbfb[:, :], x_src_ap, AF.Copy),
              waits=w_cast, inc=True)
        s_c2 = em.op("scalar", lambda s: s.activation(xsqb[:, :], x_src_ap, AF.Square),
                     inc=True)
        state["cast_mark"] = s_c2

    def emit_ln_piece(x_src_ap, dst_fn, cvec, svec_col, prow_s, prow_q, state, rows_b, pbc, t1b, xbfb, xsqb, fill_fn=None, fill_fn2=None, precast_fn=None):
        w_mm = [state["cast_mark"], misc["CG"]]
        if state.get("rows_read"):
            w_mm.append(state["rows_read"])
        for dc in range(DC):
            em.op("tensor",
                  lambda t, dc=dc: t.matmul(prow_s, ones_bf[:, :],
                                            xbfb[:, dc * 128:(dc + 1) * 128],
                                            start=(dc == 0), stop=(dc == DC - 1)),
                  waits=w_mm if dc == 0 else (), inc=(dc == DC - 1))
            em.op("tensor",
                  lambda t, dc=dc: t.matmul(prow_q, ones_bf[:, :],
                                            xsqb[:, dc * 128:(dc + 1) * 128],
                                            start=(dc == 0), stop=(dc == DC - 1)),
                  inc=(dc == DC - 1))
        t_stats = em.mark("tensor")
        state["stats_pp"] = state.get("stats_prev")
        state["stats_prev"] = t_stats
        w_row = [t_stats]
        if state.get("rows_sc_prev"):
            w_row.append(state["rows_sc_prev"])
        if state.get("bc_read"):
            w_row.append(state["bc_read"])
        em.op("scalar", lambda s: s.mul(rowa[0:1, 0:128], prow_s, 1.0 / D),
              waits=w_row, inc=True)
        em.op("scalar", lambda s: s.mul(rowa[0:1, 128:256], prow_q, 1.0 / D))
        em.drain("scalar")
        s_m2 = em.op("scalar", lambda s: s.activation(rowb[0:1, 0:128], rowa[0:1, 0:128],
                                                      AF.Square), inc=True)
        state["rows_read"] = s_m2
        v_var = em.op("vector", lambda v: v.tensor_sub(rowb[0:1, 128:256], rowa[0:1, 128:256],
                                                       rowb[0:1, 0:128]),
                      waits=[s_m2] + ([state["v_prev"]] if state.get("v_prev") else []),
                      inc=True)
        em.drain("vector")
        em.op("vector", lambda v: v.tensor_scalar_add(rowb[0:1, 128:256],
                                                      rowb[0:1, 128:256], 1e-5), inc=True)
        v_vare = em.mark("vector")
        s_sd = em.op("scalar", lambda s: s.activation(rowb[0:1, 256:384], rowb[0:1, 128:256],
                                                      AF.Sqrt),
                     waits=[v_vare], inc=True)
        v_rs = em.op("vector", lambda v: v.reciprocal(rowa[0:1, 256:384], rowb[0:1, 256:384]),
                     waits=[s_sd, state.get("bc_read")], inc=True)
        em.drain("vector")
        v_bp = em.op("vector", lambda v: v.tensor_mul(rowa[0:1, 128:256], rowa[0:1, 0:128],
                                                      rowa[0:1, 256:384]),
                     waits=[state.get("bc_read")], inc=True)
        em.drain("vector")
        state["rows_sc_prev"] = v_bp
        w_bc = [v_bp]
        if state.get("bc_prev"):
            w_bc.append(state["bc_prev"])
        if state.get("bcp_prev"):
            w_bc.append(state["bcp_prev"])
        if state.get("rowbf_free"):
            w_bc.append(state["rowbf_free"])
        s_cv = em.op("scalar", lambda s: s.copy(rowbf[:, :], rowa[0:1, 0:512]),
                     waits=w_bc, inc=True)
        if precast_fn is not None:
            precast_fn()
        if fill_fn is not None:
            fill_fn()
        g_bc = emit_bcast(rowbf[:, :], rows_b[:, :], pbc, waits=[s_cv], bf=True)
        state["rowbf_free"] = g_bc
        state["bc_read"] = g_bc
        state["bcp_prev"] = g_bc
        if fill_fn2 is not None:
            fill_fn2()
        w_ts0 = [state["mod_wait"]] if state.get("mod_wait") else []
        state["mod_wait"] = None

        def tail():
            a3 = bass.AP(rows_b, 256, [[512, 128], [0, DC], [1, 128]])
            b3 = bass.AP(rows_b, 128, [[512, 128], [0, DC], [1, 128]])
            x3 = x_src_ap.rearrange("p (c t) -> p c t", c=DC)
            t3 = t1b[:, :].rearrange("p (c t) -> p c t", c=DC)
            em.drain("vector")
            em.op("vector", lambda v: v.tensor_tensor(t3, x3, a3, ALU.mult),
                  waits=[g_bc], inc=True)
            em.drain("vector")
            em.op("vector", lambda v: v.tensor_tensor(t3, t3, b3, ALU.subtract), inc=True)
            em.drain("vector")
            for dc in range(DC):
                em.op("vector",
                      lambda v, dc=dc: v.tensor_scalar(
                          dst_fn(dc), t1b[:, dc * 128:(dc + 1) * 128],
                          cvec[:, dc:dc + 1], mod_sb[:, svec_col + dc:svec_col + dc + 1],
                          ALU.mult, ALU.add),
                      waits=w_ts0 if dc == 0 else (), inc=(dc == DC - 1))
            v_h = em.mark("vector")
            state["v_prev"] = v_h
            state["bc_prev"] = v_h
            return v_h
        return tail

    es_h = ExitStack()
    h_full = es_h.enter_context(nc.sbuf_tensor("h_full", [128, DC * 2048], BF16))

    es1 = ExitStack()
    xl = [es1.enter_context(nc.sbuf_tensor(f"xl{i}", [128, DC * 128], F32)) for i in range(2)]
    xbf = es1.enter_context(nc.sbuf_tensor("xbf", [128, DC * 128], BF16))
    xsq = es1.enter_context(nc.sbuf_tensor("xsq", [128, DC * 128], BF16))
    rows_b1 = [es1.enter_context(nc.sbuf_tensor(f"rows_b1_{i}", [128, 512], BF16))
               for i in range(2)]
    t1sb = es1.enter_context(nc.sbuf_tensor("t1sb", [128, DC * 128], F32))
    p_rs = es1.enter_context(nc.psum_tensor("p_rs", [1, 128], F32))
    p_rq = es1.enter_context(nc.psum_tensor("p_rq", [1, 128], F32))
    p_ma = es1.enter_context(nc.psum_tensor("p_ma", [1, 512], F32))
    p_ma2 = es1.enter_context(nc.psum_tensor("p_ma2", [1, 512], F32))
    p_mt = es1.enter_context(nc.psum_tensor("p_mt", [128, 1], F32))
    p_bc1 = es1.enter_context(nc.psum_tensor("p_bc1", [128, 512], F32))
    ringB = Ring(em, es1, "rB", 4, 2048, BF16)
    rowm = es1.enter_context(nc.sbuf_tensor("rowm", [1, 512], F32))

    ncg = 0
    for buf, src in ((ident_bf, d_ident), (btri_bf, d_btri), (m2_bf, d_m2),
                     (ones_bf, d_ones1), (gatew_bf, d_gatew), (sew_bf, d_sew),
                     (ones_row_bf, d_onesr)):
        em.op("gpsimd", lambda g, b=buf, s=src: g.dma_start(b[:, :], s[:, :]),
              dma_inc=(s_cg, 16))
        ncg += 16
    misc["CG"] = (s_cg, ncg)
    misc["CF"] = (s_cf, 48)
    for buf, src in ((ident_f, d_ident), (c_f32, d_cimg), (ones_row, d_onesr)):
        em.op("sync", lambda sy, b=buf, s=src: sy.dma_start(b[:, :], s[:, :]),
              dma_inc=(s_cf, 16))
    em.op("vector", lambda v: v.memset(rowa[0:1, 384:512], 0.0), inc=True)
    em.drain("vector")
    misc["v_siluc"] = em.op("scalar", lambda s: s.activation(silu_c[:, :], c_f32[:, :],
                                                             AF.Silu),
                            waits=[(s_cf, 48)], inc=True)
    em.op("sync", lambda sy: sy.dma_start(x_own[:, :], d_xo[:, :]), dma_inc=(s_xo, 16))

    if SUB1 >= 1:
        for mbh in range(8):
            emit_mod_half(mbh // 2, mbh % 2,
                          ringB, (p_ma if mbh % 2 == 0 else p_ma2)[0:1, 0:512],
                          p_mt[:, :])
        v_ca1 = em.op("vector", lambda v: v.tensor_scalar_add(ca1[:, :], mod_sb[:, 16:32],
                                                              1.0),
                      waits=[mod_evac[31]], inc=True)
        em.drain("vector")

    ln1_state = {"mod_wait": v_ca1 if SUB1 >= 1 else None}
    x_loads = [0, 0]
    x_free = [None, None]
    def ld_x(s_idx):
        slot = (s_idx // 2) % 2
        w_ld = [x_free[slot]] if x_free[slot] is not None else []
        x_loads[slot] += 1
        em.op("sync", lambda sy, sl=slot, s=s_idx: sy.dma_start(xl[sl][:, :], d_ximg[s, :, :]),
              waits=w_ld, dma_inc=(s_x[slot], 16))
        return (s_x[slot], x_loads[slot] * 16)

    x_wait = [None] * TC

    def src_of(s):
        if s % 2 == 0:
            return x_own[:, (s // 2) * 2048:(s // 2 + 1) * 2048]
        return xl[(s // 2) % 2][:, :]

    pending_tail = None
    if SUB1 >= 2:
        x_wait[1] = ld_x(1)
        emit_ln_cast(src_of(0), xbf[:, :], xsq[:, :], ln1_state,
                     extra_waits=[(s_xo, 16)])
    for s_idx in (range(TC) if SUB1 >= 2 else range(0)):
        if s_idx % 2 == 1 and s_idx + 2 < TC:
            x_wait[s_idx + 2] = ld_x(s_idx + 2)
        pc = None
        if s_idx + 1 < TC:
            nxt = s_idx + 1
            ew = [(s_xo, 16)] if nxt % 2 == 0 else [x_wait[nxt]]
            pc = (lambda nxt=nxt, ew=ew:
                  emit_ln_cast(src_of(nxt), xbf[:, :], xsq[:, :],
                               ln1_state, extra_waits=ew))

        def dst(dc, s_idx=s_idx):
            off = dc * 2048 + s_idx * 128
            return h_full[:, off:off + 128]
        fa = fb = None
        if SUB1 >= 1:
            mbh = 8 + s_idx
            pma_h = (p_ma if s_idx % 2 == 0 else p_ma2)[0:1, 0:512]
            fa = (lambda mbh=mbh, pma_h=pma_h:
                  emit_mod_half(mbh // 2, mbh % 2, ringB, pma_h, p_mt[:, :], stage="a"))
            fb = (lambda mbh=mbh, pma_h=pma_h:
                  emit_mod_half(mbh // 2, mbh % 2, ringB, pma_h, p_mt[:, :], stage="b"))
        t_cl = emit_ln_piece(src_of(s_idx), dst, ca1, 0, p_rs[0:1, 0:128],
                             p_rq[0:1, 0:128], ln1_state, rows_b1[s_idx % 2],
                             p_bc1[:, :], t1sb, xbf, xsq, fill_fn=fa, fill_fn2=fb,
                             precast_fn=pc)
        if pending_tail is not None:
            vh_prev = pending_tail()
            if (s_idx - 1) % 2 == 1:
                x_free[((s_idx - 1) // 2) % 2] = vh_prev
        pending_tail = t_cl
    if SUB1 >= 2:
        pending_tail()
    if SUB1 >= 1 and PHASES >= 2:
        emit_clamp(32, gacl)

    em.run_block()
    es1.close()

    if PHASES < 2:
        em.op("sync", lambda sy: sy.dma_start(d_out[:, :], x_own[:, :]),
              waits=[(s_xo, 16)], dma_inc=(s_out, 16))
        em.op("sync", lambda sy: sy.wait_ge(s_out, 16))
        em.run_block()
        es_h.close()
        es.close()
        return nc

    es2 = ExitStack()
    z_all = es2.enter_context(nc.sbuf_tensor("z_all", [128, NH * 1024], BF16))
    k_sb = es2.enter_context(nc.sbuf_tensor("k_sb", [128, 2048], BF16))
    vfm = es2.enter_context(nc.sbuf_tensor("vfm", [128, 2048], BF16))
    vtok = es2.enter_context(nc.sbuf_tensor("vtok", [128, 2048], BF16))
    q_sb = es2.enter_context(nc.sbuf_tensor("q_sb", [128, 1024], BF16))
    pt_sb = [es2.enter_context(nc.sbuf_tensor(f"pt{i}", [128, 1024], BF16)) for i in range(4)]
    negm_b = es2.enter_context(nc.sbuf_tensor("negm_b", [128, 512], BF16))
    rl_b = es2.enter_context(nc.sbuf_tensor("rl_b", [128, 512], BF16))
    m4 = es2.enter_context(nc.sbuf_tensor("m4", [128, 8], F32))
    mtmp = es2.enter_context(nc.sbuf_tensor("mtmp", [128, 1], F32))
    pacc = es2.enter_context(nc.psum_tensor("pacc", [128, 1024], F32))
    pstp = es2.enter_context(nc.psum_tensor("pstp", [128, 1024], F32))
    plr = es2.enter_context(nc.psum_tensor("plr", [1, 512], F32))
    pz = es2.enter_context(nc.psum_tensor("pz", [128, 512], F32))
    ptv = [es2.enter_context(nc.psum_tensor(f"ptv{i}", [128, 128], BF16)) for i in range(2)]

    g = {"pacc": [None, None], "pst": [None, None], "ptv": [None, None],
         "plr": None, "pz": None, "negm": None, "rl": None, "rowa2": None,
         "k": None, "v": None, "vt": None, "q": None, "pp": [None, None],
         "ptsb": [None, None, None, None], "zread": None}

    def proj_quarters(dst_sb, w_units, n_q, moving_fn, evac_scale, prev_use, qtr_base=0):
        last = None
        for qtr0 in range(n_q):
            qtr = qtr_base + qtr0
            bank = qtr % 2
            w0 = [w_units[0][1], w_units[1][1]]
            if g["pacc"][bank] is not None:
                w0.append(g["pacc"][bank])
            for dc in range(DC):
                ubuf = w_units[dc // 8][0]
                em.op("tensor",
                      lambda t, ub=ubuf, dc=dc, qtr=qtr, bank=bank:
                      t.matmul(pacc[:, bank * 512:bank * 512 + 512],
                               ub[:, (dc % 8) * 128:(dc % 8) * 128 + 128],
                               moving_fn(dc, qtr),
                               start=(dc == 0), stop=(dc == DC - 1)),
                      waits=w0 if dc == 0 else (), inc=(dc == DC - 1))
            tm = em.mark("tensor")
            w_ev = [tm]
            if qtr0 < 2 and prev_use is not None:
                w_ev.append(prev_use)
            ev = em.op("scalar",
                       lambda s, qtr=qtr, bank=bank: s.activation(
                           dst_sb[:, qtr * 512:qtr * 512 + 512],
                           pacc[:, bank * 512:bank * 512 + 512],
                           AF.Copy, scale=evac_scale),
                       waits=w_ev, inc=True)
            g["pacc"][bank] = ev
            last = ev
        return last

    stats_rot = [0]
    stat_guard = [None, None]

    def emit_stats_job(job, grp, kq_done, q_done):
        (li, j) = job
        l = grp * 4 + li
        ncols = (2 * l + 2) * 128
        c0 = j * 512
        cw = min(512, ncols - c0)
        bank = stats_rot[0] % 2
        stats_rot[0] += 1
        w0 = [q_done, kq_done]
        if stat_guard[bank] is not None:
            w0.append(stat_guard[bank])
        if stats_rot[0] <= 2:
            w0.append(g["pp"][0])
            w0.append(g["pp"][1])
        em.op("tensor",
              lambda t, l=l, c0=c0, cw=cw, bank=bank: t.matmul(
                  pstp[:, bank * 512:bank * 512 + cw],
                  q_sb[:, l * 128:l * 128 + 128],
                  k_sb[:, c0:c0 + cw], start=True, stop=True),
              waits=w0, inc=True)
        tm = em.mark("tensor")
        col = l
        if j == 0:
            rv = em.op("vector",
                       lambda v, col=col, cw=cw, bank=bank: v.reduce_max(
                           m4[:, col:col + 1], pstp[:, bank * 512:bank * 512 + cw],
                           axis=AX.X),
                       waits=[tm, g.get("m4col")], inc=True)
        else:
            em.op("vector",
                  lambda v, cw=cw, bank=bank: v.reduce_max(
                      mtmp[:, :], pstp[:, bank * 512:bank * 512 + cw], axis=AX.X),
                  waits=[tm], inc=True)
            em.drain("vector")
            rv = em.op("vector",
                       lambda v, col=col: v.tensor_max(m4[:, col:col + 1],
                                                       m4[:, col:col + 1], mtmp[:, :]),
                       inc=True)
        em.drain("vector")
        stat_guard[bank] = rv
        return rv

    def stats_jobs_for(grp):
        jobs = []
        for li in range(4):
            l = grp * 4 + li
            nj = ((2 * l + 2) * 128 + 511) // 512
            for j in range(nj):
                jobs.append((li, j))
        return jobs

    pair_rot = [0]
    pt_rot = [0]

    def emit_pair_front(h, grp, R, p, negm_row, negm_done, vtok_done):
        tca, tcb = 2 * p, 2 * p + 1
        pr = pair_rot[0] % 2
        pti = pt_rot[0] % 4
        pair_rot[0] += 1
        pt_rot[0] += 1
        ptens = pstp if pr == 0 else pacc
        ptb = pt_sb[pti]
        def q0_of(tc):
            li = max(0, tc // 2 - 4 * grp)
            return li * 128
        q0a, q0b = q0_of(tca), q0_of(tcb)
        w_st = [negm_done, misc["CG"]]
        if g["pp"][pr] is not None:
            w_st.append(g["pp"][pr])
        if p == 0:
            w_st += [stat_guard[0], stat_guard[1]]
        if p <= 1 and pr == 1:
            w_st += [g["pacc"][0], g["pacc"][1]]
        for k, q0 in ((0, q0a), (1, q0b)):
            em.op("tensor",
                  lambda t, k=k, q0=q0, ptens=ptens, negm_row=negm_row: t.matmul(
                      ptens[:, k * 512 + q0:k * 512 + 512], ones_row_bf[:, :],
                      negm_row[0:1, q0:512], start=True, stop=False),
                  waits=w_st if k == 0 else (), inc=False)
        for k, tc, q0 in ((0, tca, q0a), (1, tcb, q0b)):
            em.op("tensor",
                  lambda t, tc=tc, k=k, q0=q0, ptens=ptens, grp=grp: t.matmul(
                      ptens[:, k * 512 + q0:k * 512 + 512],
                      k_sb[:, tc * 128:tc * 128 + 128],
                      q_sb[:, grp * 512 + q0:grp * 512 + 512], start=False, stop=True),
                  waits=(), inc=(k == 1))
        t_st = em.mark("tensor")
        w_exp = [t_st]
        if g["ptsb"][pti] is not None:
            w_exp.append(g["ptsb"][pti])
        em.op("scalar",
              lambda s, ptb=ptb, ptens=ptens, q0=q0a: s.activation(
                  ptb[:, q0:512], ptens[:, q0:512], AF.Exp),
              waits=w_exp, inc=(q0b >= 512))
        if q0b < 512:
            em.op("scalar",
                  lambda s, ptb=ptb, ptens=ptens, q0=q0b: s.activation(
                      ptb[:, 512 + q0:1024], ptens[:, 512 + q0:1024], AF.Exp),
                  inc=True)
        s_exp = em.mark("scalar")
        g["pp"][pr] = s_exp
        if pr == 1:
            g["pacc"][0] = s_exp
            g["pacc"][1] = s_exp
        last_dep = s_exp
        for k, tc in ((0, tca), (1, tcb)):
            for li in range(4):
                l = grp * 4 + li
                c0 = k * 512 + li * 128
                if tc == 2 * l:
                    last_dep = em.op("vector",
                                     lambda v, ptb=ptb, c0=c0: v.tensor_mul(
                                         ptb[:, c0:c0 + 128], ptb[:, c0:c0 + 128],
                                         btri_bf[:, :]),
                                     waits=[s_exp], inc=True)
                    em.drain("vector")
                elif tc == 2 * l + 1:
                    last_dep = em.op("vector",
                                     lambda v, ptb=ptb, c0=c0: v.tensor_mul(
                                         ptb[:, c0:c0 + 128], ptb[:, c0:c0 + 128],
                                         m2_bf[:, :]),
                                     waits=[s_exp], inc=True)
                    em.drain("vector")
        return (p, pti, ptb, last_dep, tca, tcb, q0a, q0b)

    def emit_pair_back(ctx, R, vtok_done):
        (p, pti, ptb, last_dep, tca, tcb, q0a, q0b) = ctx
        grp_off = {}
        w_l = [last_dep]
        if p == 0:
            if g["plr"] is not None:
                w_l.append(g["plr"])
        first = True
        for k, q0 in ((0, q0a), (1, q0b)):
            if q0 >= 512:
                continue
            em.op("tensor",
                  lambda t, ptb=ptb, k=k, q0=q0, p=p, R=R: t.matmul(
                      plr[0:1, q0:512], ones_bf[:, :],
                      ptb[:, k * 512 + q0:k * 512 + 512],
                      start=(p == 0 and k == 0), stop=(p == R // 2 - 1 and k == 1)),
                  waits=w_l if first else (), inc=(k == 1))
            first = False
        w_av = []
        if p == 0:
            w_av.append(vtok_done)
            if g["pz"] is not None:
                w_av.append(g["pz"])
        first = True
        for k, tc, q0 in ((0, tca, q0a), (1, tcb, q0b)):
            if q0 >= 512:
                continue
            em.op("tensor",
                  lambda t, ptb=ptb, k=k, tc=tc, q0=q0, p=p, R=R: t.matmul(
                      pz[:, q0:512], vtok[:, tc * 128:tc * 128 + 128],
                      ptb[:, k * 512 + q0:k * 512 + 512],
                      start=(p == 0 and k == 0), stop=(p == R // 2 - 1 and k == 1)),
                  waits=w_av if first else (), inc=(k == 1))
            first = False
        t_av = em.mark("tensor")
        g["ptsb"][pti] = t_av
        return t_av

    for h in range(NH):
        units = {}
        for nm, dten in (("wk", d_wk), ("wq", d_wq), ("wv", d_wv)):
            us = []
            for half in range(2):
                slot, buf, ld = ringA.push(dten[h, :, half * 1024:half * 1024 + 1024])
                us.append([buf, ld, slot])
            units[nm] = us
        kq_done = proj_quarters(
            k_sb, units["wk"], 4,
            lambda dc, qtr: h_full[:, dc * 2048 + qtr * 512: dc * 2048 + qtr * 512 + 512],
            1.0, g["k"])
        for u in units["wk"]:
            ringA.consumed(u[2])
        q_done = proj_quarters(
            q_sb, units["wq"], 2,
            lambda dc, qh: bass.AP(h_full, dc * 2048 + qh * 1024,
                                   [[DC * 2048, 128], [256, 4], [1, 128]]),
            SCALE, g["q"])
        for u in units["wq"]:
            ringA.consumed(u[2])
        stats_rot[0] = 0
        all_jobs = [(0, j) for j in stats_jobs_for(0)] + [(1, j) for j in stats_jobs_for(1)]
        jit = iter(all_jobs)
        stat_red = {}

        def pop_job():
            it = next(jit, None)
            if it is not None:
                grp_, job_ = it
                stat_red[grp_] = emit_stats_job(job_, grp_, kq_done, q_done)

        vq_done = None
        for qtr in range(4):
            pop_job()
            pop_job()
            vq_done = proj_quarters(
                vfm, units["wv"], 1,
                lambda dc, q: h_full[:, dc * 2048 + q * 512: dc * 2048 + q * 512 + 512],
                1.0, g["v"] if qtr == 0 else None, qtr_base=qtr)
        for u in units["wv"]:
            ringA.consumed(u[2])
        vt_last = None
        for tc in range(TC):
            pop_job()
            bk = tc % 2
            w_tr = [vq_done] if tc < 2 else []
            if g["ptv"][bk] is not None:
                w_tr.append(g["ptv"][bk])
            em.op("tensor",
                  lambda t, tc=tc, bk=bk: t.transpose(
                      ptv[bk][:, :], vfm[:, tc * 128:tc * 128 + 128], ident_bf[:, :]),
                  waits=w_tr, inc=True)
            ttr = em.mark("tensor")
            w_ev = [ttr]
            if tc < 2 and g["vt"] is not None:
                w_ev.append(g["vt"])
            ev = em.op("scalar",
                       lambda s, tc=tc, bk=bk: s.copy(vtok[:, tc * 128:tc * 128 + 128],
                                                      ptv[bk][:, :]),
                       waits=w_ev, inc=True)
            g["ptv"][bk] = ev
            vt_last = ev
        for grp_, job_ in jit:
            stat_red[grp_] = emit_stats_job(job_, grp_, kq_done, q_done)

        pend_q = []

        def grp_tail(grp, t_av_last, nrow):
            w_rc = [t_av_last]
            if g["rowa2"] is not None:
                w_rc.append(g["rowa2"])

            def _recip(v, nrow=nrow):
                with nc.allow_low_precision(reason="1/l multiplier applied to bf16 z"):
                    return v.reciprocal(nrow[:, :], plr[0:1, 0:512])
            v_rec = em.op("vector", _recip,
                          waits=w_rc + [g["ngrow" + str(grp)]], inc=True)
            em.drain("vector")
            g["plr"] = v_rec
            bk = grp
            w_rl = [v_rec, g["pacc"][bk]]
            if g["rl"] is not None:
                w_rl.append(g["rl"])
            if g["pp"][1] is not None:
                w_rl.append(g["pp"][1])
            g_rl = emit_bcast(nrow[:, :], rl_b[:, :],
                              pacc[:, bk * 512:bk * 512 + 512], waits=w_rl, bf=True)
            g["pp"][1] = g_rl
            g["pacc"][bk] = g_rl
            g["rowa2"] = g_rl
            g["ngrow" + str(grp)] = g_rl
            v_z = em.op("vector",
                        lambda v, h=h, grp=grp: v.tensor_mul(
                            z_all[:, h * 1024 + grp * 512: h * 1024 + grp * 512 + 512],
                            pz[:, :], rl_b[:, :]),
                        waits=[t_av_last, g_rl], inc=True)
            em.drain("vector")
            g["pz"] = v_z
            g["rl"] = v_z
            g["zlast"] = v_z

        tail_due = []

        def pop_back():
            (grp0, R0, nrow0, ctx) = pend_q.pop(0)
            t_av = emit_pair_back(ctx, R0, vt_last)
            if not any(e[0] == grp0 for e in pend_q):
                g["ngrow" + str(grp0)] = em.mark("tensor")
                tail_due.append((grp0, t_av, nrow0))

        def flush_tails():
            while tail_due:
                (grp0, t_av, nrow0) = tail_due.pop(0)
                grp_tail(grp0, t_av, nrow0)

        for grp in range(2):
            R = 8 * grp + 8
            nrow = rowbf if grp == 0 else rowbf2
            last_red = stat_red[grp]
            pair_rot[0] = 0
            w_tr = [last_red, stat_guard[0]]
            if g["pp"][0] is not None:
                w_tr.append(g["pp"][0])
            for li in range(4):
                em.op("tensor",
                      lambda t, li=li, grp=grp: t.transpose(
                          pstp[0:1, li * 128:li * 128 + 128],
                          m4[:, grp * 4 + li:grp * 4 + li + 1], ident_f[:, :]),
                      waits=w_tr if li == 0 else (), inc=(li == 3))
            t_mr = em.mark("tensor")
            g["m4col"] = t_mr
            w_ng = [t_mr]
            if g.get("ngrow" + str(grp)) is not None:
                w_ng.append(g["ngrow" + str(grp)])
            s_ng = em.op("scalar", lambda s, nrow=nrow: s.mul(nrow[:, :],
                                                             pstp[0:1, 0:512], -1.0),
                         waits=w_ng, inc=True)
            g["pp"][0] = s_ng
            for p in range(R // 2):
                ctx = emit_pair_front(h, grp, R, p, nrow, s_ng, vt_last)
                pend_q.append((grp, R, nrow, ctx))
                flush_tails()
                while len(pend_q) > 3:
                    pop_back()
        while pend_q:
            pop_back()
        flush_tails()
        g["k"] = em.mark("tensor")
        g["v"] = em.mark("tensor")
        g["vt"] = em.mark("tensor")
        g["q"] = em.mark("tensor")

    zs_done = g["zlast"]
    op_guard = [None, None, None, None]
    first_doc_wait = [(s_xo, 16), zs_done]
    for doc in range(DC):
        slot, buf, ld = ringA.push(d_wo[doc, :, 0:1024])
        slot2, buf2, ld2 = ringA.push(d_wo[doc, :, 1024:2048])
        pair = doc % 2
        for qh in range(2):
            bi = pair * 2 + qh
            tgt = (pacc[:, qh * 512:qh * 512 + 512] if pair == 0
                   else pstp[:, qh * 512:qh * 512 + 512])
            w0 = [ld, ld2] + ([op_guard[bi]] if op_guard[bi] is not None else [])
            if doc < 2:
                w0 += first_doc_wait + [g["pacc"][0], g["pacc"][1],
                                        g["pp"][0], g["pp"][1]]
            for hh in range(NH):
                ub = buf if hh < 8 else buf2
                em.op("tensor",
                      lambda t, ub=ub, hh=hh, qh=qh, tgt=tgt: t.matmul(
                          tgt, ub[:, (hh % 8) * 128:(hh % 8) * 128 + 128],
                          z_all[:, hh * 1024 + qh * 512: hh * 1024 + qh * 512 + 512],
                          start=(hh == 0), stop=(hh == NH - 1)),
                      waits=w0 if hh == 0 else (), inc=(hh == NH - 1))
            tm = em.mark("tensor")
            x_ap = bass.AP(x_own, (4 * qh) * 2048 + doc * 128,
                           [[LC * 2048, 128], [2048, 4], [1, 128]])
            tgt3 = tgt.rearrange("p (a b) -> p a b", a=4)
            ev = em.op("vector",
                       lambda v, tgt3=tgt3, x_ap=x_ap, doc=doc: v.scalar_tensor_tensor(
                           x_ap, tgt3, gacl[:, doc:doc + 1], x_ap, ALU.mult, ALU.add),
                       waits=[tm], inc=True)
            em.drain("vector")
            op_guard[bi] = ev
        ringA.consumed(slot)
        ringA.consumed(slot2)
    x_attn_done = em.mark("vector")

    if PHASES < 3:
        em.op("sync", lambda sy: sy.dma_start(d_out[:, :], x_own[:, :]),
              waits=[x_attn_done, (s_xo, 16)], dma_inc=(s_out, 16))
        em.op("sync", lambda sy: sy.wait_ge(s_out, 16))
        em.run_block()
        es2.close()
        es_h.close()
        es.close()
        return nc

    em.run_block()
    es2.close()
    es_h.close()

    es3 = ExitStack()
    h2 = es3.enter_context(nc.sbuf_tensor("h2", [128, DC * 1024], BF16))
    x1s = es3.enter_context(nc.sbuf_tensor("x1s", [128, HC * 1024], BF16))
    ysb = es3.enter_context(nc.sbuf_tensor("ysb", [128, HC * 1024], BF16))
    xbf2 = es3.enter_context(nc.sbuf_tensor("xbf2", [128, DC * 128], BF16))
    xsq2 = es3.enter_context(nc.sbuf_tensor("xsq2", [128, DC * 128], BF16))
    t1f = es3.enter_context(nc.sbuf_tensor("t1f", [128, DC * 128], F32))
    rows_b2 = [es3.enter_context(nc.sbuf_tensor(f"rows_b2_{i}", [128, 512], BF16))
               for i in range(2)]
    w3c = es3.enter_context(nc.sbuf_tensor("w3c", [128, 1], F32))
    w4c = es3.enter_context(nc.sbuf_tensor("w4c", [128, 1], F32))
    comb_set = [[es3.enter_context(nc.sbuf_tensor(f"comb_{s}_{i}", [128, 512], BF16))
                 for i in range(2)] for s in range(2)]
    gst = es3.enter_context(nc.sbuf_tensor("gst", [128, LC * NE], F32))
    grow = es3.enter_context(nc.sbuf_tensor("grow", [8, 1024], F32))
    gtmp = es3.enter_context(nc.sbuf_tensor("gtmp", [128, NE], F32))
    w1c = es3.enter_context(nc.sbuf_tensor("w1c", [128, 1], F32))
    w2c = es3.enter_context(nc.sbuf_tensor("w2c", [128, 1], F32))
    eq2 = es3.enter_context(nc.sbuf_tensor("eq2", [128, NE], F32))
    rowc0 = es3.enter_context(nc.sbuf_tensor("rowc0", [1, 512], BF16))
    rowc1 = es3.enter_context(nc.sbuf_tensor("rowc1", [1, 512], BF16))
    rowd0 = es3.enter_context(nc.sbuf_tensor("rowd0", [1, 512], BF16))
    rowd1 = es3.enter_context(nc.sbuf_tensor("rowd1", [1, 512], BF16))
    s_rd = es3.enter_context(nc.semaphore("s_rd"))
    ringD = Ring(em, es3, "rD", 18, 512, BF16)
    px1 = es3.enter_context(nc.psum_tensor("px1", [128, 1024], F32))
    px3 = es3.enter_context(nc.psum_tensor("px3", [128, 1024], F32))
    pdn = [es3.enter_context(nc.psum_tensor(f"pdn{i}", [128, 512], F32)) for i in range(3)]
    pm2 = es3.enter_context(nc.psum_tensor("pm2", [128, 512], F32))

    v_cb1 = em.op("vector", lambda v: v.tensor_scalar_add(cb1[:, :], mod_sb[:, 64:80], 1.0),
                  waits=[mod_evac[79]], inc=True)
    em.drain("vector")
    emit_clamp(80, gbcl)
    v_gbcl = [em.mark("vector")]

    ln2_state = {"mod_wait": v_cb1}
    emit_ln_cast(x_own[:, 0:2048], xbf2[:, :], xsq2[:, :], ln2_state)
    pending_t2 = None
    for lc in range(LC):
        src = x_own[:, lc * 2048:(lc + 1) * 2048]
        pc2 = None
        if lc + 1 < LC:
            pc2 = (lambda lc=lc: emit_ln_cast(
                x_own[:, (lc + 1) * 2048:(lc + 2) * 2048],
                xbf2[:, :], xsq2[:, :], ln2_state))

        def dst2(dc, lc=lc):
            off = dc * 1024 + lc * 128
            return h2[:, off:off + 128]
        t_cl2 = emit_ln_piece(src, dst2, cb1, 48, pdn[0][0:1, 0:128],
                              pdn[1][0:1, 0:128], ln2_state, rows_b2[lc % 2],
                              pdn[2][:, :], t1f, xbf2, xsq2, precast_fn=pc2)
        if pending_t2 is not None:
            pending_t2()
        pending_t2 = t_cl2
    pending_t2()
    h2_done = em.mark("vector")

    for half in range(2):
        for dc in range(DC):
            em.op("tensor",
                  lambda t, dc=dc, half=half: t.matmul(
                      px1[0:8, half * 512:half * 512 + 512],
                      gatew_bf[:, dc * 8:dc * 8 + 8],
                      h2[:, dc * 1024 + half * 512: dc * 1024 + half * 512 + 512],
                      start=(dc == 0), stop=(dc == DC - 1)),
                  waits=[h2_done, misc["CG"]] if dc == 0 else (), inc=(dc == DC - 1))
    t_gsc = em.mark("tensor")
    s_gev = em.op("scalar", lambda s: s.copy(grow[:, :], px1[0:8, 0:1024]),
                  waits=[t_gsc], inc=True)
    prev_cp = s_gev
    for lc in range(LC):
        em.op("tensor",
              lambda t, lc=lc: t.transpose(px3[:, 0:8],
                                           grow[0:8, lc * 128:lc * 128 + 128],
                                           ident_f[0:8, 0:8]),
              waits=[prev_cp], inc=True)
        t_gt = em.mark("tensor")
        prev_cp = em.op("scalar",
                        lambda s, lc=lc: s.copy(gst[:, lc * 8:lc * 8 + 8], px3[:, 0:8]),
                        waits=[t_gt], inc=True)
    s_gst = em.mark("scalar")

    comb_rows_done = None
    prev_tct = None
    for lc in range(LC):
        sl = gst[:, lc * 8:lc * 8 + 8]
        em.op("vector", lambda v, sl=sl: v.reduce_max(w1c[:, :], sl, axis=AX.X),
              waits=[s_gst] if lc == 0 else (), inc=True)
        em.drain("vector")
        em.op("vector", lambda v, sl=sl: v.tensor_scalar(gtmp[:, :], sl, w1c[:, :], None,
                                                         ALU.is_equal),
              waits=[prev_tct], inc=True)
        em.drain("vector")
        em.op("vector", lambda v, sl=sl: v.scalar_tensor_tensor(eq2[:, :], gtmp[:, :],
                                                                -1.0e30, sl, ALU.mult,
                                                                ALU.add), inc=True)
        em.drain("vector")
        em.op("vector", lambda v: v.reduce_max(w2c[:, :], eq2[:, :], axis=AX.X), inc=True)
        em.drain("vector")
        v_d = em.op("vector", lambda v: v.tensor_sub(w3c[:, :], w1c[:, :], w2c[:, :]),
                    waits=[prev_tct], inc=True)
        em.drain("vector")
        s_w1 = em.op("scalar", lambda s: s.activation(w4c[:, :], w3c[:, :], AF.Sigmoid),
                     waits=[v_d, prev_tct], inc=True)
        em.op("vector", lambda v: v.tensor_scalar(eq2[:, :], eq2[:, :], w2c[:, :], None,
                                                  ALU.is_equal), inc=True)
        em.drain("vector")
        v_w2 = em.op("vector", lambda v: v.tensor_scalar(w3c[:, :], w4c[:, :], -1.0,
                                                         1.0, ALU.mult, ALU.add),
                     waits=[s_w1], inc=True)
        em.drain("vector")
        em.op("vector", lambda v: v.tensor_scalar(gtmp[:, :], gtmp[:, :], w4c[:, :], None,
                                                  ALU.mult), waits=[s_w1], inc=True)
        em.drain("vector")
        em.op("vector", lambda v: v.tensor_scalar(eq2[:, :], eq2[:, :], w3c[:, :], None,
                                                  ALU.mult), waits=[v_w2], inc=True)
        em.drain("vector")
        v_cb = em.op("vector", lambda v: v.tensor_add(gtmp[:, :], gtmp[:, :], eq2[:, :]),
                     inc=True)
        em.drain("vector")
        em.op("tensor",
              lambda t, lc=lc: t.transpose(px3[0:8, 0:128], gtmp[:, :], ident_f[:, :]),
              waits=[v_cb, prev_cp], inc=True)
        t_ct = em.mark("tensor")
        prev_tct = t_ct
        comb_rows_done = em.op("scalar",
                               lambda s, lc=lc: s.copy(grow[:, lc * 128:lc * 128 + 128],
                                                       px3[0:8, 0:128]),
                               waits=[t_ct], inc=True)
        prev_cp = comb_rows_done
    for half in range(2):
        for dc in range(DC):
            em.op("tensor",
                  lambda t, dc=dc, half=half: t.matmul(
                      px1[0:1, half * 512:half * 512 + 512],
                      sew_bf[:, dc:dc + 1],
                      h2[:, dc * 1024 + half * 512: dc * 1024 + half * 512 + 512],
                      start=(dc == 0), stop=(dc == DC - 1)),
                  waits=[s_gev] if (dc == 0 and half == 0) else (), inc=(dc == DC - 1))
    t_se = em.mark("tensor")
    em.op("scalar", lambda s: s.activation(rowc0[:, :], px1[0:1, 0:512],
                                           AF.Sigmoid), waits=[t_se], inc=True)
    s_sg = em.op("scalar", lambda s: s.activation(rowc1[:, :], px1[0:1, 512:1024],
                                                  AF.Sigmoid), inc=True)

    def push_gu(e2, hc2):
        gu = []
        for proj in range(2):
            us = []
            for half in range(2):
                if e2 >= NE:
                    dten = d_shg if proj == 0 else d_shu
                    srcp = dten[(e2 - NE) * 8 + hc2, :, half * 1024:half * 1024 + 1024]
                else:
                    dten = d_wg if proj == 0 else d_wu
                    srcp = dten[e2, hc2, :, half * 1024:half * 1024 + 1024]
                us.append(ringA.push(srcp))
            gu.append(us)
        return gu

    gu_pend = [push_gu(0, 0)]
    dpre_next = {0: None}
    ln2_rows_done = em.mark("scalar")
    misc["pm2_free"] = em.mark("scalar")
    gx = {"px1": s_sg, "px3": comb_rows_done, "comb_ready": {},
          "pdn": [ln2_rows_done, ln2_rows_done, ln2_rows_done],
          "x1s": None, "ysb": None, "comb": None}
    y_marks = {}
    pdn_rot = [0]
    for e in range(NEU):
        shared = e >= NE
        cset = 0 if e == 9 else e % 2
        if e == 0:
            w_cb = [comb_rows_done, s_sg, gx["px1"]]
            misc["n_rd"] = misc.get("n_rd", 0) + 32
            em.op("gpsimd", lambda gy: gy.dma_start(rowd0[:, :], grow[0:1, 0:512]),
                  waits=w_cb, dma_inc=(s_rd, 16))
            em.op("gpsimd", lambda gy: gy.dma_start(rowd1[:, :], grow[0:1, 512:1024]),
                  dma_inc=(s_rd, 16))
            b0 = emit_bcast(rowd0[:, :], comb_set[0][0][:, :], px1[:, 0:512],
                            waits=[(s_rd, misc["n_rd"])], bf=True)
            g_cb = emit_bcast(rowd1[:, :], comb_set[0][1][:, :], px1[:, 0:512],
                              waits=[b0], bf=True)
            gx["px1"] = g_cb
        else:
            g_cb = gx["comb_ready"][cset]
        def push_down(q, e=e, shared=shared):
            us = []
            for hc in range(HC):
                if shared:
                    srcap = d_shd[(e - NE) * 8 + hc, :, q * 512:q * 512 + 512]
                else:
                    srcap = d_wd[e, hc, :, q * 512:q * 512 + 512]
                us.append(ringD.push(srcap))
            return us
        dpre = {}
        if dpre_next.get(0) is not None:
            dpre[0] = dpre_next[0]
            dpre_next[0] = None
        for hc in range(HC):
            gu = gu_pend[0]
            if hc == HC - 2 and dpre.get(0) is None:
                dpre[0] = push_down(0)
            w1_ = [gu[0][0][2], gu[0][1][2], h2_done]
            if gx["px1"] is not None:
                w1_.append(gx["px1"])
            for dc in range(DC):
                ub = gu[0][dc // 8][1]
                for half in range(2):
                    em.op("tensor",
                          lambda t, ub=ub, dc=dc, half=half: t.matmul(
                              px1[:, half * 512:half * 512 + 512],
                              ub[:, (dc % 8) * 128:(dc % 8) * 128 + 128],
                              h2[:, dc * 1024 + half * 512: dc * 1024 + half * 512 + 512],
                              start=(dc == 0), stop=(dc == DC - 1)),
                          waits=w1_ if (dc == 0 and half == 0) else (),
                          inc=(dc == DC - 1 and half == 1))
            t_x1 = em.mark("tensor")
            ringA.consumed(gu[0][0][0])
            ringA.consumed(gu[0][1][0])
            w_si = [t_x1]
            if y_marks.get(hc) is not None:
                w_si.append(y_marks[hc])
            s_si = em.op("scalar",
                         lambda s, hc=hc: s.activation(x1s[:, hc * 1024:hc * 1024 + 1024],
                                                       px1[:, :], AF.Silu),
                         waits=w_si, inc=True)
            gx["px1"] = s_si
            w3_ = [gu[1][0][2], gu[1][1][2]]
            if gx["px3"] is not None:
                w3_.append(gx["px3"])
            for dc in range(DC):
                ub = gu[1][dc // 8][1]
                for half in range(2):
                    em.op("tensor",
                          lambda t, ub=ub, dc=dc, half=half: t.matmul(
                              px3[:, half * 512:half * 512 + 512],
                              ub[:, (dc % 8) * 128:(dc % 8) * 128 + 128],
                              h2[:, dc * 1024 + half * 512: dc * 1024 + half * 512 + 512],
                              start=(dc == 0), stop=(dc == DC - 1)),
                          waits=w3_ if (dc == 0 and half == 0) else (),
                          inc=(dc == DC - 1 and half == 1))
            t_x3 = em.mark("tensor")
            ringA.consumed(gu[1][0][0])
            ringA.consumed(gu[1][1][0])
            nxt = (e, hc + 1) if hc < HC - 1 else (e + 1, 0)
            if nxt[0] < NEU:
                gu_pend[0] = push_gu(nxt[0], nxt[1])
            w_y = [t_x3, s_si, g_cb]
            if gx["ysb"] is not None and hc == 0:
                w_y.append(gx["ysb"])
            em.op("vector",
                  lambda v, hc=hc: v.tensor_mul(ysb[:, hc * 1024:hc * 1024 + 1024],
                                                x1s[:, hc * 1024:hc * 1024 + 1024],
                                                px3[:, :]),
                  waits=w_y, inc=True)
            em.drain("vector")
            em.op("vector",
                  lambda v, hc=hc, cset=cset: v.tensor_mul(
                      ysb[:, hc * 1024:hc * 1024 + 512],
                      ysb[:, hc * 1024:hc * 1024 + 512],
                      comb_set[cset][0][:, :]), inc=True)
            em.drain("vector")
            v_y = em.op("vector",
                        lambda v, hc=hc, cset=cset: v.tensor_mul(
                            ysb[:, hc * 1024 + 512:hc * 1024 + 1024],
                            ysb[:, hc * 1024 + 512:hc * 1024 + 1024],
                            comb_set[cset][1][:, :]), inc=True)
            em.drain("vector")
            gx[("comb_used", cset)] = v_y
            gx["px3"] = v_y
            y_marks[hc] = v_y
        gx["comb"] = em.mark("vector")
        y_done = em.mark("vector")
        en = e + 1
        if en <= NE:
            nset = en % 2
            w_cb = [gx.get(("comb_used", nset))]
            if misc.get("g_cb_prev") is not None:
                w_cb.append(misc["g_cb_prev"])
            w_cb.append(misc["pm2_free"])
            if en < NE:
                misc["n_rd"] = misc.get("n_rd", 0) + 32
                em.op("gpsimd",
                      lambda gy, en=en: gy.dma_start(rowd0[:, :], grow[en:en + 1, 0:512]),
                      waits=w_cb + [comb_rows_done], dma_inc=(s_rd, 16))
                em.op("gpsimd",
                      lambda gy, en=en: gy.dma_start(rowd1[:, :],
                                                     grow[en:en + 1, 512:1024]),
                      dma_inc=(s_rd, 16))
                b0 = emit_bcast(rowd0[:, :], comb_set[nset][0][:, :], pm2[:, 0:512],
                                waits=[(s_rd, misc["n_rd"])], bf=True)
                g_nc = emit_bcast(rowd1[:, :], comb_set[nset][1][:, :], pm2[:, 0:512],
                                  waits=[b0], bf=True)
            else:
                b0 = emit_bcast(rowc0[:, :], comb_set[nset][0][:, :], pm2[:, 0:512],
                                waits=w_cb + [s_sg], bf=True)
                g_nc = emit_bcast(rowc1[:, :], comb_set[nset][1][:, :], pm2[:, 0:512],
                                  waits=[b0], bf=True)
            misc["g_cb_prev"] = g_nc
            gx["comb_ready"][nset] = g_nc
        for q in range(4):
            if q < 3:
                dpre[q + 1] = push_down(q + 1)
            elif e + 1 < NEU:
                dpre[0] = None
            dunits = dpre[q]
            for dloc in range(4):
                doc = q * 4 + dloc
                for th in range(2):
                    rot = pdn_rot[0] % 3
                    pdn_rot[0] += 1
                    wd0 = [u[2] for u in dunits] + [y_done]
                    if gx["pdn"][rot] is not None:
                        wd0.append(gx["pdn"][rot])
                    if e == 0 and doc == 0 and th == 0:
                        wd0 += [x_attn_done, (s_xo, 16), v_gbcl[0]]
                    for hc in range(HC):
                        em.op("tensor",
                              lambda t, hc=hc, dloc=dloc, th=th, rot=rot, ub=dunits[hc][1]:
                              t.matmul(pdn[rot][:, :],
                                       ub[:, dloc * 128:dloc * 128 + 128],
                                       ysb[:, hc * 1024 + th * 512: hc * 1024 + th * 512 + 512],
                                       start=(hc == 0), stop=(hc == HC - 1)),
                              waits=wd0 if hc == 0 else (), inc=(hc == HC - 1))
                    tdm = em.mark("tensor")
                    x_ap = bass.AP(x_own, (4 * th) * 2048 + doc * 128,
                                   [[LC * 2048, 128], [2048, 4], [1, 128]])
                    p3 = pdn[rot][:, :].rearrange("p (a b) -> p a b", a=4)
                    evd = em.op("vector",
                                lambda v, p3=p3, x_ap=x_ap, doc=doc: v.scalar_tensor_tensor(
                                    x_ap, p3, gbcl[:, doc:doc + 1], x_ap, ALU.mult, ALU.add),
                                waits=[tdm], inc=True)
                    em.drain("vector")
                    gx["pdn"][rot] = evd
            for u in dunits:
                ringD.consumed(u[0])
            if q == 3 and e + 1 < NEU:
                def push_down_next(e2=e + 1):
                    us = []
                    for hc in range(HC):
                        if e2 >= NE:
                            srcap = d_shd[(e2 - NE) * 8 + hc, :, 0:512]
                        else:
                            srcap = d_wd[e2, hc, :, 0:512]
                        us.append(ringD.push(srcap))
                    return us
                dpre_next[0] = push_down_next()
        gx["x1s"] = em.mark("vector")
        gx["x1s"] = em.mark("vector")
        gx["ysb"] = em.mark("tensor")

    x_done = em.mark("vector")
    em.op("sync", lambda sy: sy.dma_start(d_out[:, :], x_own[:, :]),
          waits=[x_done], dma_inc=(s_out, 16))
    em.op("sync", lambda sy: sy.wait_ge(s_out, 16))

    em.run_block()
    es3.close()
    es.close()
    return nc


def kernel(**inputs):
    shared = pack_shared(inputs)
    x = np.asarray(inputs["L0_attn_x"])
    c_all = np.asarray(inputs["L0_c"])
    in_maps = []
    for core in range(NCORE):
        m = dict(shared)
        m.update(pack_core(x, c_all, core))
        in_maps.append(m)
    nc = build_nc()
    res = run_bass_kernel_spmd(nc, in_maps, core_ids=list(range(NCORE)))
    return unpack_out(res.results)
